# version 06
# speedup vs baseline: 1.2351x; 1.0011x over previous
import contextlib
import numpy as np
import ml_dtypes
import concourse.bass as bass
import concourse.mybir as mybir
from concourse.bass_utils import run_bass_kernel_spmd

F32 = mybir.dt.float32
BF16 = mybir.dt.bfloat16
I32 = mybir.dt.int32
AF = mybir.ActivationFunctionType
ALU = mybir.AluOpType
AX = mybir.AxisListType.X
f32 = np.float32

B, S, D = 2, 8192, 1024
H, HD = 16, 64
NSLAB = 8
SLAB = S // NSLAB
TPC = 2 * SLAB
NT = TPC // 128
E, DFF, FDFF = 16, 512, 1024
EPS_LN = 1e-5
EPS_RMS = 1e-6
MSHIFT = 30.0
ROPE_BASE = 10000.0

PHASES = ("attn_a", "ffn1", "attn_b", "ffn2")
ORDER_B = [(sl, c) for c in range(8) for sl in range(2)]
AA_STAGES = 4
AA_NPB = 8
AA_PP = 15
DBG = ()


def _bf(x):
    return np.ascontiguousarray(x).astype(ml_dtypes.bfloat16)


def _owner(s_idx):
    j = min(s_idx, 7 - s_idx)
    return j, (0 if s_idx == j else 1)



def prep_in_maps(inputs):
    x = np.asarray(inputs["L0_attn_a_x_BSE"], np.float32)
    wq_a = np.asarray(inputs["L0_attn_a_w_q"], np.float32).reshape(D, H * HD)
    wk_a = np.asarray(inputs["L0_attn_a_w_k"], np.float32).reshape(D, H * HD)
    wv_a = np.asarray(inputs["L0_attn_a_w_v"], np.float32).reshape(D, H * HD)
    wo_a = np.ascontiguousarray(
        np.asarray(inputs["L0_attn_a_w_o"], np.float32).reshape(D, H * HD).T)

    wq_b = np.asarray(inputs["L0_attn_b_wq"], np.float32)
    wk_b = np.asarray(inputs["L0_attn_b_wk"], np.float32)
    wv_b = np.asarray(inputs["L0_attn_b_wv"], np.float32)
    wo_b = np.asarray(inputs["L0_attn_b_wo"], np.float32)
    wg1 = np.asarray(inputs["L0_attn_b_wg1"], np.float32)
    wg2 = np.asarray(inputs["L0_attn_b_wg2"], np.float32)
    norm_w = np.asarray(inputs["L0_attn_b_norm_w"], np.float32)

    gate_w = np.asarray(inputs["L0_ffn_gate_w"], np.float32)
    eb = np.asarray(inputs["L0_ffn_expert_bias"], np.float32)
    w1 = np.asarray(inputs["L0_ffn_w1"], np.float32)
    b1 = np.asarray(inputs["L0_ffn_b1"], np.float32)
    w2 = np.asarray(inputs["L0_ffn_w2"], np.float32)
    b2 = np.asarray(inputs["L0_ffn_b2"], np.float32)
    w3 = np.asarray(inputs["L0_ffn_w3"], np.float32)
    b3 = np.asarray(inputs["L0_ffn_b3"], np.float32)
    sw1 = np.asarray(inputs["L0_ffn_sw1"], np.float32)
    sb1 = np.asarray(inputs["L0_ffn_sb1"], np.float32)
    sw2 = np.asarray(inputs["L0_ffn_sw2"], np.float32)
    sb2 = np.asarray(inputs["L0_ffn_sb2"], np.float32)
    sw3 = np.asarray(inputs["L0_ffn_sw3"], np.float32)
    sb3 = np.asarray(inputs["L0_ffn_sb3"], np.float32)

    w1t = np.zeros(((E + 1) * D, FDFF), np.float32)
    w3t = np.zeros(((E + 1) * D, FDFF), np.float32)
    w2t = np.zeros((FDFF + E * DFF, D), np.float32)
    w1t[0:D, :] = sw1.T
    w3t[0:D, :] = sw3.T
    w2t[0:FDFF, :] = sw2.T
    for e in range(E):
        w1t[(1 + e) * D:(2 + e) * D, 0:DFF] = w1[e].T
        w3t[(1 + e) * D:(2 + e) * D, 0:DFF] = w3[e].T
        w2t[FDFF + e * DFF:FDFF + (e + 1) * DFF, :] = w2[e].T
    b1cols = np.zeros((128, 72), np.float32)
    b3cols = np.zeros((128, 72), np.float32)
    for m in range(8):
        b1cols[:, m] = sb1[m * 128:(m + 1) * 128]
        b3cols[:, m] = sb3[m * 128:(m + 1) * 128]
    for e in range(E):
        for m in range(4):
            b1cols[:, 8 + e * 4 + m] = b1[e, m * 128:(m + 1) * 128]
            b3cols[:, 8 + e * 4 + m] = b3[e, m * 128:(m + 1) * 128]
    b2rows = np.concatenate([sb2[None, :], b2], axis=0)
    gate_wt = np.ascontiguousarray(gate_w.T)
    ebrow = eb.reshape(1, E)

    def dup(w):
        wr = w.reshape(D, H, HD)
        return np.ascontiguousarray(
            np.concatenate([wr, wr], axis=2).reshape(D, H * 2 * HD))
    wq_b_dup = dup(wq_b)
    wk_b_dup = dup(wk_b)
    nw_cols = np.ascontiguousarray(norm_w.reshape(8, 128).T)

    dmask = np.zeros((4, 128, 512), np.float32)
    qq = np.arange(512)[None, :]
    kk = np.arange(128)[:, None]
    for d_ in range(4):
        dmask[d_] = (128 * d_ + kk <= qq)
    trimask = (np.arange(128)[:, None] <= np.arange(128)[None, :]).astype(np.float32)
    ident = np.eye(128, dtype=np.float32)

    shared = dict(
        w1t=w1t, w3t=w3t, w2t=w2t,
        b1cols=b1cols, b3cols=b3cols, b2rows=b2rows,
        gate_wt=gate_wt, ebrow=ebrow,
        wq_b_dup=wq_b_dup, wk_b_dup=wk_b_dup, wv_b=wv_b, wo_b=wo_b,
        wg1=wg1, wg2=wg2, nw_cols=nw_cols, wo_a=wo_a,
        dmask=_bf(dmask), trimask=_bf(trimask), ident=_bf(ident),
    )

    theta = (ROPE_BASE ** (-2.0 / (H * HD) * np.arange(H * HD, dtype=np.float64))
             ).reshape(H, HD)
    in_maps = []
    for c in range(8):
        seq, j = c // 4, c % 4
        slabs = (j, 7 - j)
        tok_rows = np.concatenate(
            [np.arange(sl * SLAB, (sl + 1) * SLAB) for sl in slabs])
        x_own = np.ascontiguousarray(x[seq, tok_rows, :])

        hsel = slice(2 * c * HD, (2 * c + 2) * HD)
        wq_a_sl = np.ascontiguousarray(wq_a[:, hsel])
        wk_a_sl = np.ascontiguousarray(wk_a[:, hsel])
        wv_a_sl = np.ascontiguousarray(wv_a[:, hsel])

        pos = tok_rows.astype(np.float64)
        ang = pos[None, None, :] * theta[:, :, None]
        ropecs = np.empty((H, 128, TPC), np.float64)
        ropecs[:, :HD] = np.cos(ang)
        ropecs[:, HD:] = np.sin(ang)
        ropecs = ropecs.astype(np.float32)

        oag_base = np.zeros((1, 2), np.int32)
        for i2, sl in enumerate(slabs):
            oag_base[0, i2] = (seq * S + sl * SLAB) * 130
        gla_state_off = np.zeros((1, 8), np.int32)
        rank_rows = 2 * H * 128
        for s_idx in range(NSLAB):
            j2, half = _owner(s_idx)
            gla_state_off[0, s_idx] = ((4 * seq + j2) * rank_rows + half * H * 128) * 65
        my_prefix_off = np.zeros((1, 2), np.int32)
        for i, sl in enumerate(slabs):
            my_prefix_off[0, i] = sl * 128 * 1040

        m = dict(shared)
        m.update(x_own=x_own, wq_a_sl=wq_a_sl, wk_a_sl=wk_a_sl, wv_a_sl=wv_a_sl,
                 ropecs=ropecs, oag_base=oag_base, gla_state_off=gla_state_off,
                 my_prefix_off=my_prefix_off)
        in_maps.append(m)
    return in_maps



class Sem:
    __slots__ = ("h", "v")

    def __init__(self, h):
        self.h = h
        self.v = 0

    def inc(self, ins, n=1):
        ins.then_inc(self.h, n)
        self.v += n
        return self.v

    def wait(self, eng, val=None):
        eng.wait_ge(self.h, self.v if val is None else val)


class Kb:
    def __init__(self, nc, ctx):
        self.nc = nc
        self.ctx = ctx

    def sem(self, name):
        return Sem(self.ctx.enter_context(self.nc.semaphore(name)))

    def sb(self, ctx, name, shape, dtype):
        return ctx.enter_context(self.nc.sbuf_tensor(name, shape, dtype))

    def ps(self, ctx, name, shape, dtype=F32):
        return ctx.enter_context(self.nc.psum_tensor(name, shape, dtype))


def build_nc(phases=None):
    phases = PHASES if phases is None else phases
    nc = bass.Bass("TRN2", target_bir_lowering=False)

    t_x = nc.dram_tensor("x_own", [TPC, D], F32, kind="ExternalInput")
    t_y = nc.dram_tensor("y", [TPC, D], F32, kind="ExternalOutput")
    di = {}
    for nm, shp, dt in [
        ("wq_a_sl", [D, 128], F32), ("wk_a_sl", [D, 128], F32),
        ("wv_a_sl", [D, 128], F32), ("wo_a", [D, D], F32),
        ("wq_b_dup", [D, 2048], F32), ("wk_b_dup", [D, 2048], F32),
        ("wv_b", [D, D], F32), ("wo_b", [D, D], F32),
        ("wg1", [D, 64], F32), ("wg2", [64, D], F32), ("nw_cols", [128, 8], F32),
        ("w1t", [(E + 1) * D, FDFF], F32), ("w3t", [(E + 1) * D, FDFF], F32),
        ("w2t", [FDFF + E * DFF, D], F32),
        ("b1cols", [128, 72], F32), ("b3cols", [128, 72], F32), ("b2rows", [17, D], F32),
        ("gate_wt", [D, 16], F32), ("ebrow", [1, E], F32),
        ("ropecs", [H, 128, TPC], F32),
        ("dmask", [4, 128, 512], BF16), ("trimask", [128, 128], BF16),
        ("ident", [128, 128], BF16),
        ("oag_base", [1, 2], I32), ("gla_state_off", [1, 8], I32),
        ("my_prefix_off", [1, 2], I32),
    ]:
        di[nm] = nc.dram_tensor(nm, shp, dt, kind="ExternalInput")

    dscr = dict(
        cc_ln1_in=nc.dram_tensor("cc_ln1_in", [TPC, D], BF16, kind="Internal"),
        cc_ln1_out=nc.dram_tensor("cc_ln1_out", [8 * TPC, D], BF16, kind="Internal", addr_space="Shared"),
        cc_oatt_in=nc.dram_tensor("cc_oatt_in", [2 * S, 130], BF16, kind="Internal"),
        cc_oatt_out=nc.dram_tensor("cc_oatt_out", [8 * 2 * S, 130], BF16, kind="Internal", addr_space="Shared"),
        aa_owno=nc.dram_tensor("aa_owno", [2 * 8 * SLAB, 130], BF16, kind="Internal"),
        cc_gla_in=nc.dram_tensor("cc_gla_in", [2 * H * 128, 65], F32, kind="Internal"),
        cc_gla_out=nc.dram_tensor("cc_gla_out", [8 * 2 * H * 128, 65], F32, kind="Internal", addr_space="Shared"),
        gla_prefix=nc.dram_tensor("gla_prefix", [NSLAB * 128, 1040], F32, kind="Internal"),
        gla_kp=nc.dram_tensor("gla_kp", [H * 128, TPC], BF16, kind="Internal"),
        gla_el=nc.dram_tensor("gla_el", [H * 128, TPC], BF16, kind="Internal"),
        gla_o=nc.dram_tensor("gla_o", [TPC, D], BF16, kind="Internal"),
    )

    with contextlib.ExitStack() as ctx:
        kb = Kb(nc, ctx)
        kb.pe = kb.sem("g_pe")
        kb.dve = kb.sem("g_dve")
        kb.act = kb.sem("g_act")
        kb.cc = kb.sem("g_cc")
        kb.gx = kb.sem("g_gx")
        kb.sx = kb.sem("g_sx")
        kb.g0 = kb.sem("g_g0")
        kb.g1 = kb.sem("g_g1")
        kb.s0 = kb.sem("g_s0")
        kb.s1 = kb.sem("g_s1")
        kb.out = kb.sem("g_out")

        kb.x_res = kb.sb(ctx, "x_res", [128, NT * D], F32)
        kb.ident = kb.sb(ctx, "c_ident", [128, 128], BF16)
        kb.trimask = kb.sb(ctx, "c_trimask", [128, 128], BF16)
        kb.dmask = kb.sb(ctx, "c_dmask", [128, 4 * 512], BF16)
        kb.ones_row = kb.sb(ctx, "c_ones_row", [1, 128], BF16)
        kb.ones_f = kb.sb(ctx, "c_ones_f", [128, SLAB], F32)
        kb.b1cols = kb.sb(ctx, "c_b1cols", [128, 72], F32)
        kb.b3cols = kb.sb(ctx, "c_b3cols", [128, 72], F32)
        kb.nw = kb.sb(ctx, "c_nw", [128, 8], F32)
        kb.offs = kb.sb(ctx, "c_offs", [1, 32], I32)
        kb.stats = kb.sb(ctx, "c_stats", [128, 128], F32)
        kb.ro = ctx.enter_context(nc.gpsimd.register("g_ro"))
        kb.junk = kb.sb(ctx, "c_junk", [128, D], F32)
        kb.junkv = kb.sb(ctx, "c_junkv", [128, 64], F32)

        build_setup(kb, t_x, di)
        if "attn_a" in phases:
            build_attn_a(kb, di, dscr)
        if "ffn1" in phases:
            build_ffn(kb, di, "f1")
        if "attn_b" in phases:
            build_attn_b(kb, di, dscr)
        if "ffn2" in phases:
            build_ffn(kb, di, "f2")
        build_store(kb, t_y)
    return nc


def _dbg_out(kb, name, src_ap, shape, dtype):
    nc = kb.nc
    t = nc.dram_tensor(name, shape, dtype, kind="ExternalOutput")
    with nc.Block("dbg_" + name) as blk:
        @blk.sync
        def _(s):
            kb.out.inc(s.dma_start(t.ap(), src_ap), 16)
            kb.out.wait(s)


def build_setup(kb, t_x, di):
    nc = kb.nc
    with nc.Block("setup") as blk:
        @blk.sync
        def _(s):
            kb.sx.inc(s.dma_start(
                kb.x_res.ap().rearrange("p (t c) -> p t c", c=D),
                t_x.ap().rearrange("(t p) c -> p t c", p=128)), 16)
            kb.sx.inc(s.dma_start(kb.ident[:, :], di["ident"].ap()), 16)
            kb.sx.inc(s.dma_start(kb.trimask[:, :], di["trimask"].ap()), 16)
            kb.sx.inc(s.dma_start(
                kb.dmask.ap().rearrange("p (d q) -> p d q", d=4),
                di["dmask"].ap().rearrange("d p q -> p d q")), 16)
            kb.sx.inc(s.dma_start(kb.b1cols[:, :], di["b1cols"].ap()), 16)
            kb.sx.inc(s.dma_start(kb.b3cols[:, :], di["b3cols"].ap()), 16)
            kb.sx.inc(s.dma_start(kb.nw[:, :], di["nw_cols"].ap()), 16)
            kb.sx.inc(s.dma_start(kb.offs[:, 0:2], di["oag_base"].ap()), 16)
            kb.sx.inc(s.dma_start(kb.offs[:, 16:24], di["gla_state_off"].ap()), 16)
            kb.sx.inc(s.dma_start(kb.offs[:, 24:26], di["my_prefix_off"].ap()), 16)
            kb.sx.wait(s)
        @blk.vector
        def _(v):
            v.memset(kb.ones_row[:, :], 1.0)
            v.memset(kb.ones_f[:, :], 1.0)


def emit_ln(kb, tag, ln_tm, lnT=None, pst=None, ship=None, strace=None):
    nc = kb.nc
    stats, junk, x_res = kb.stats, kb.junk, kb.x_res
    pe0, dve0, act0 = kb.pe.v, kb.dve.v, kb.act.v
    s0v, s1v = kb.s0.v, kb.s1.v
    with nc.Block(f"ln_{tag}") as blk:
        @blk.vector
        def _(v):
            for t in range(NT):
                ins = v.reduce_sum(stats[:, t:t + 1], x_res[:, t * D:(t + 1) * D],
                                   axis=AX)
            kb.dve.inc(ins)
            kb.dve.wait(v)
            kb.dve.inc(v.tensor_scalar(stats[:, 32:48], stats[:, 0:16],
                                       1.0 / D, None, ALU.mult))
            kb.dve.wait(v)
            kb.dve.inc(v.scalar_tensor_tensor(stats[:, 48:64], stats[:, 32:48],
                                              -1.0, stats[:, 32:48],
                                              ALU.mult, ALU.mult))
            kb.dve.wait(v)
            kb.act.wait(v, act0 + NT + 2)
            kb.dve.inc(v.scalar_tensor_tensor(stats[:, 64:80], stats[:, 16:32],
                                              1.0 / D, stats[:, 48:64],
                                              ALU.mult, ALU.add))
            kb.dve.wait(v)
            kb.dve.inc(v.tensor_copy(kb.junkv[:, 0:16], stats[:, 64:80]))
            kb.act.wait(v, act0 + NT + 5)
            kb.dve.inc(v.scalar_tensor_tensor(stats[:, 112:128], stats[:, 32:48],
                                              -1.0, stats[:, 96:112],
                                              ALU.mult, ALU.mult))
            kb.dve.wait(v)
            for t in range(NT):
                if ship is not None and t >= 2:
                    sl = kb.s0 if t % 2 == 0 else kb.s1
                    slv = s0v if t % 2 == 0 else s1v
                    sl.wait(v, slv + 16 * (t // 2))
                kb.dve.inc(v.tensor_scalar(
                    ln_tm[:, (t % 2) * D:(t % 2) * D + D],
                    x_res[:, t * D:(t + 1) * D],
                    stats[:, 96 + t:97 + t], stats[:, 112 + t:113 + t],
                    ALU.mult, ALU.add))
                if lnT is not None and t > 0:
                    kb.pe.wait(v, pe0 + t)
                    kb.dve.inc(v.tensor_copy(
                        lnT.ap().rearrange("p (k n) -> p k n", k=8)[:, :, (t - 1) * 128:t * 128],
                        pst[(t - 1) % 2][:, :].rearrange("p (k n) -> p k n", k=8)))
            if lnT is not None:
                kb.pe.wait(v, pe0 + NT)
                kb.dve.inc(v.tensor_copy(
                    lnT.ap().rearrange("p (k n) -> p k n", k=8)[:, :, (NT - 1) * 128:NT * 128],
                    pst[(NT - 1) % 2][:, :].rearrange("p (k n) -> p k n", k=8)))

        @blk.scalar
        def _(sc):
            for t in range(NT):
                if t > 0:
                    kb.act.wait(sc)
                kb.act.inc(sc.activation(junk[:, :], x_res[:, t * D:(t + 1) * D],
                                         AF.Square, accum_out=stats[:, 16 + t:17 + t]))
            kb.act.wait(sc)
            kb.act.inc(sc.activation(junk[:, 0:16], x_res[:, 0:16], AF.Square))
            kb.act.wait(sc)
            kb.act.inc(sc.activation(junk[:, 0:16], x_res[:, 0:16], AF.Square))
            kb.dve.wait(sc, dve0 + 5)
            kb.act.inc(sc.activation(stats[:, 80:96], stats[:, 64:80], AF.Ln,
                                     bias=f32(EPS_LN)))
            kb.act.wait(sc)
            kb.act.inc(sc.activation(stats[:, 96:112], stats[:, 80:96], AF.Exp,
                                     scale=f32(-0.5)))
            kb.act.wait(sc)
            kb.act.inc(sc.activation(junk[:, 0:16], x_res[:, 0:16], AF.Square))

        if lnT is not None:
            @blk.tensor
            def _(t_):
                napply = 0
                for t in range(NT):
                    kb.dve.wait(t_, dve0 + 6 + (2 * t + 1 if t > 0 else 1))
                    if t >= 2:
                        kb.dve.wait(t_, dve0 + 6 + 2 * (t - 1) + 2)
                    for k in range(8):
                        ins = t_.transpose(
                            pst[t % 2][:, k * 128:(k + 1) * 128],
                            ln_tm[:, (t % 2) * D + k * 128:(t % 2) * D + (k + 1) * 128],
                            kb.ident[:, :])
                        if k == 7:
                            kb.pe.inc(ins)
        if ship is not None:
            @blk.sync
            def _(s):
                for t in range(NT):
                    kb.dve.wait(s, dve0 + 6 + t + 1)
                    sl = kb.s0 if t % 2 == 0 else kb.s1
                    sl.inc(s.dma_start(
                        ship.ap()[t * 128:(t + 1) * 128, :],
                        ln_tm[:, (t % 2) * D:(t % 2) * D + D]), 16)
                kb.s0.wait(s)
                kb.s1.wait(s)



def build_attn_a(kb, di, dscr):
    nc = kb.nc
    cc_ln1_in, cc_ln1_out = dscr["cc_ln1_in"], dscr["cc_ln1_out"]
    cc_oatt_in, cc_oatt_out = dscr["cc_oatt_in"], dscr["cc_oatt_out"]

    with contextlib.ExitStack() as pctx:
        ln_tm = kb.sb(pctx, "aa_lntm", [128, 2 * D], BF16)
        strace = kb.sb(pctx, "aa_strace", [128, NT * 10], F32) if "strace" in DBG else None
        emit_ln(kb, "aa", ln_tm, ship=cc_ln1_in, strace=strace)
        if strace is not None:
            _dbg_out(kb, "dbg_strace", strace.ap(), [128, NT * 10], F32)
        with nc.Block("aa_ag1") as blk:
            @blk.gpsimd
            def _(g):
                kb.cc.inc(g.collective_compute(
                    "AllGather", ALU.bypass, replica_groups=[list(range(8))],
                    ins=[cc_ln1_in.ap().opt()], outs=[cc_ln1_out.ap().opt()]))
                kb.cc.wait(g)
        if "ln1" in DBG:
            _dbg_out(kb, "dbg_ln1", cc_ln1_in.ap(), [TPC, D], BF16)
        if AA_STAGES < 2:
            return

    with contextlib.ExitStack() as pctx:
        Q2 = kb.sb(pctx, "aa_Q2", [128, S], BF16)
        K2 = kb.sb(pctx, "aa_K2", [128, S], BF16)
        V2 = kb.sb(pctx, "aa_V2", [128, 64 * 130], BF16)
        O2 = kb.sb(pctx, "aa_O2", [128, 64 * 130], F32)
        lnTb = kb.sb(pctx, "aa_lnTb", [128, 2 * 8192], BF16)
        wqs = kb.sb(pctx, "aa_wqs", [128, 8 * 128], BF16)
        wks = kb.sb(pctx, "aa_wks", [128, 8 * 128], BF16)
        wvs = kb.sb(pctx, "aa_wvs", [128, 8 * 128], BF16)
        P_sb = kb.sb(pctx, "aa_Psb", [128, 4 * 1024], BF16)

        with nc.Block("aa_wload") as blk:
            @blk.gpsimd
            def _(g):
                for nm, dst in (("wq_a_sl", wqs), ("wk_a_sl", wks), ("wv_a_sl", wvs)):
                    kb.gx.inc(g.dma_start(
                        dst.ap().rearrange("p (k n) -> p k n", k=8),
                        di[nm].ap().rearrange("(k p) n -> p k n", p=128)), 16)
                kb.gx.wait(g)
            @blk.vector
            def _(v):
                v130 = V2.ap().rearrange("p (t c) -> p t c", c=130)
                kb.dve.inc(v.memset(V2[:, :], 0.0))
                kb.dve.wait(v)
                v.memset(v130[:, :, 64:65], 1.0)
                v.memset(v130[:, :, 129:130], 1.0)
                v.memset(O2[:, :], 0.0)

        for unit in range(2 if AA_STAGES >= 3 else 1):
            with contextlib.ExitStack() as qctx:
                pq = kb.ps(qctx, f"aa_pq{unit}", [128, 1024], F32)
                pk = kb.ps(qctx, f"aa_pk{unit}", [128, 1024], F32)
                pv = [kb.ps(qctx, f"aa_pv{unit}{i}", [128, 512], F32) for i in range(2)]
                emit_attn_proj(kb, unit, cc_ln1_out, lnTb, wqs, wks, wvs,
                               Q2, K2, V2, pq, pk, pv)
            if AA_STAGES < 3:
                if "qkv" in DBG:
                    _dbg_out(kb, "dbg_q2", Q2.ap(), [128, S], BF16)
                    _dbg_out(kb, "dbg_k2", K2.ap(), [128, S], BF16)
                    _dbg_out(kb, "dbg_v2", V2.ap(), [128, 64 * 130], BF16)
                return
            with contextlib.ExitStack() as qctx:
                pS = [kb.ps(qctx, f"aa_pS{unit}{i}", [128, 1024], F32) for i in range(2)]
                pO = [kb.ps(qctx, f"aa_pO{unit}{i}", [128, 512], F32) for i in range(2)]
                emit_attn_unit(kb, unit, Q2, K2, V2, O2, P_sb, pS, pO, cc_oatt_in)

        if "oatt" in DBG:
            _dbg_out(kb, "dbg_oatt", cc_oatt_in.ap(), [2 * S, 130], BF16)
        if AA_STAGES < 4:
            return
        with nc.Block("aa_oag") as blk:
            @blk.gpsimd
            def _(g):
                kb.gx.wait(g)
                kb.cc.inc(g.collective_compute(
                    "AllGather", ALU.bypass, replica_groups=[list(range(8))],
                    ins=[cc_oatt_in.ap().opt()], outs=[cc_oatt_out.ap().opt()]))
                kb.cc.wait(g)

    with contextlib.ExitStack() as pctx:
        emit_attn_outproj(kb, pctx, di, cc_oatt_out, dscr["aa_owno"])


def emit_attn_proj(kb, unit, cc_ln1_out, lnTb, wqs, wks, wvs, Q2, K2, V2, pq, pk, pv):
    nc = kb.nc

    def issue_loads(s, b):
        j2, half = _owner(b)
        row0 = (4 * unit + j2) * TPC + half * SLAB
        slot = kb.s0 if b % 2 == 0 else kb.s1
        for kchunk in range(8):
            slot.inc(s.dma_start(
                lnTb[:, (b % 2) * 8192 + kchunk * 1024:(b % 2) * 8192 + (kchunk + 1) * 1024],
                cc_ln1_out.ap()[row0:row0 + SLAB, kchunk * 128:(kchunk + 1) * 128],
                transpose=True), 16)
        return slot.v

    tgt = {}
    with nc.Block(f"aa_u{unit}_pl") as blk:
        @blk.sync
        def _(s):
            tgt[0] = issue_loads(s, 0)
    for b in range(AA_NPB):
        pe0, dve0 = kb.pe.v, kb.dve.v
        with nc.Block(f"aa_u{unit}_p{b}") as blk:
            @blk.sync
            def _(s, b=b):
                if b + 1 < AA_NPB:
                    tgt[b + 1] = issue_loads(s, b + 1)
            @blk.tensor
            def _(t_, b=b):
                slot = kb.s0 if b % 2 == 0 else kb.s1
                slot.wait(t_, tgt[b])
                lb = lnTb[:, (b % 2) * 8192:(b % 2) * 8192 + 8192]
                ins = None
                for half2 in range(2):
                    for k in range(8):
                        t_.matmul(pq[:, half2 * 512:half2 * 512 + 512],
                                  wqs[:, k * 128:(k + 1) * 128],
                                  lb[:, k * 1024 + half2 * 512:k * 1024 + half2 * 512 + 512],
                                  start=(k == 0), stop=(k == 7))
                    for k in range(8):
                        ins = t_.matmul(pk[:, half2 * 512:half2 * 512 + 512],
                                        wks[:, k * 128:(k + 1) * 128],
                                        lb[:, k * 1024 + half2 * 512:k * 1024 + half2 * 512 + 512],
                                        start=(k == 0), stop=(k == 7))
                kb.pe.inc(ins)
                for tt in range(8):
                    for k in range(8):
                        ins = t_.matmul(pv[tt // 4][:, (tt % 4) * 128:(tt % 4) * 128 + 128],
                                        lb[:, k * 1024 + tt * 128:k * 1024 + (tt + 1) * 128],
                                        wvs[:, k * 128:(k + 1) * 128],
                                        start=(k == 0), stop=(k == 7))
                    if tt == 3 or tt == 7:
                        kb.pe.inc(ins)
            @blk.vector
            def _(v, b=b, pe0=pe0):
                kb.pe.wait(v, pe0 + 1)
                v.tensor_copy(Q2[:, b * 1024:(b + 1) * 1024], pq[:, :])
                kb.dve.inc(v.tensor_copy(K2[:, b * 1024:(b + 1) * 1024], pk[:, :]))
                for g2 in range(2):
                    kb.pe.wait(v, pe0 + 2 + g2)
                    for tt4 in range(4):
                        gt = b * 8 + g2 * 4 + tt4
                        v.tensor_copy(V2[:, gt * 130:gt * 130 + 64],
                                      pv[g2][:, tt4 * 128:tt4 * 128 + 64])
                        ins = v.tensor_copy(V2[:, gt * 130 + 65:gt * 130 + 129],
                                            pv[g2][:, tt4 * 128 + 64:tt4 * 128 + 128])
                    kb.dve.inc(ins)


def emit_attn_unit(kb, unit, Q2, K2, V2, O2, P_sb, pS, pO, cc_oatt_in):
    nc = kb.nc
    NQG = S // 512
    for qg in range(NQG):
        nk = 4 * qg + 4
        pe0, act0, dve0 = kb.pe.v, kb.act.v, kb.dve.v
        with nc.Block(f"aa_u{unit}_q{qg}") as blk:
            @blk.tensor
            def _(t_, qg=qg, nk=nk, pe0=pe0, act0=act0, dve0=dve0):
                for s in range(nk + 2):
                    ins = None
                    if s < nk:
                        kt = s
                        ps_ = pS[kt % 2]
                        if kt >= 2:
                            kb.act.wait(t_, act0 + kt - 1)
                        for h in range(2):
                            ins = t_.matmul(ps_[:, h * 512:h * 512 + 512],
                                            K2[h * 64:h * 64 + 64, kt * 128:(kt + 1) * 128],
                                            Q2[h * 64:h * 64 + 64, qg * 512:(qg + 1) * 512],
                                            start=True, stop=True)
                    if s >= 2:
                        kt = s - 2
                        d = kt - 4 * qg
                        if d >= 0:
                            kb.dve.wait(t_, dve0 + d + 1)
                        else:
                            kb.act.wait(t_, act0 + kt + 1)
                        pb = P_sb[:, (kt % 4) * 1024:(kt % 4) * 1024 + 1024]
                        for h in range(2):
                            for qt in range(4):
                                ins = t_.matmul(
                                    pO[qt // 2][:, (qt % 2) * 256 + h * 65:(qt % 2) * 256 + h * 65 + 65],
                                    pb[:, h * 512 + qt * 128:h * 512 + (qt + 1) * 128],
                                    V2[:, kt * 130 + h * 65:kt * 130 + h * 65 + 65],
                                    start=(kt == 0 and h == 0 and qt % 2 == 0),
                                    stop=(kt == nk - 1 and h == 1 and qt % 2 == 1))
                    kb.pe.inc(ins)

            @blk.scalar
            def _(sc, nk=nk, pe0=pe0):
                for kt in range(nk):
                    ps_ = pS[kt % 2]
                    kb.pe.wait(sc, pe0 + kt + 1)
                    kb.act.inc(sc.activation(
                        P_sb[:, (kt % 4) * 1024:(kt % 4) * 1024 + 1024],
                        ps_[:, :], AF.Exp, bias=f32(-MSHIFT), scale=f32(0.125)))

            @blk.vector
            def _(v, qg=qg, nk=nk, act0=act0, pe0=pe0):
                for d in range(4):
                    kt = 4 * qg + d
                    kb.act.wait(v, act0 + kt + 1)
                    pb = P_sb[:, (kt % 4) * 1024:(kt % 4) * 1024 + 1024]
                    for h in range(2):
                        ins = v.tensor_tensor(
                            pb[:, h * 512:h * 512 + 512],
                            pb[:, h * 512:h * 512 + 512],
                            kb.dmask[:, d * 512:(d + 1) * 512], ALU.mult)
                        if h == 1:
                            kb.dve.inc(ins)
                kb.pe.wait(v, pe0 + nk + 2)
                for qt in range(4):
                    gt = qg * 4 + qt
                    kb.dve.inc(v.tensor_tensor(
                        O2[:, gt * 130:(gt + 1) * 130], O2[:, gt * 130:(gt + 1) * 130],
                        pO[qt // 2][:, (qt % 2) * 256:(qt % 2) * 256 + 130], ALU.add))

    with nc.Block(f"aa_ship_o{unit}") as blk:
        @blk.gpsimd
        def _(g):
            kb.gx.inc(g.dma_start(
                cc_oatt_in.ap()[unit * S:(unit + 1) * S, :].rearrange(
                    "(t p) c -> p t c", p=128),
                O2.ap().rearrange("p (t c) -> p t c", c=130)), 16)
            kb.gx.wait(g)
    if unit == 0:
        with nc.Block("aa_zero") as blk:
            @blk.vector
            def _(v):
                v.memset(O2[:, :], 0.0)


def emit_attn_outproj(kb, pctx, di, cc_oatt_out, owno):
    nc = kb.nc
    woa = kb.sb(pctx, "aa_woa", [128, 8 * 1024], BF16)
    oasm = kb.sb(pctx, "aa_oasm", [128, 2 * 8 * 130], F32)
    osc = kb.sb(pctx, "aa_osc", [128, 1024], BF16)
    ofm = kb.sb(pctx, "aa_ofm", [128, 1024], BF16)
    rcp = kb.sb(pctx, "aa_rcp", [128, 16], F32)
    pt = kb.ps(pctx, "aa_pt", [128, 1024], BF16)
    po = kb.ps(pctx, "aa_po", [128, 1024], F32)

    with nc.Block("aa_wo_load") as blk:
        @blk.gpsimd
        def _(g):
            kb.gx.inc(g.dma_start(
                woa.ap().rearrange("p (k n) -> p k n", k=8),
                di["wo_a"].ap().rearrange("(k p) n -> p k n", p=128)), 16)
            for sl in range(2):
                g.reg_load(kb.ro, kb.offs[0:1, sl:sl + 1])
                kb.gx.inc(g.dma_start(
                    owno.ap()[sl * 8 * SLAB:(sl + 1) * 8 * SLAB, :].rearrange(
                        "(s r) c -> s r c", s=8),
                    bass.AP(cc_oatt_out, kb.ro,
                            [[2 * S * 130, 8], [130, SLAB], [1, 130]])), 16)
            kb.gx.wait(g)

    pe0, dve0, gx0 = kb.pe.v, kb.dve.v, kb.gx.v
    NDV = 4
    with nc.Block("aa_op") as blk:
        @blk.gpsimd
        def _(g):
            for t in range(NT):
                sl, tt = t // 8, t % 8
                base = sl * 8 * SLAB + tt * 128
                if t > 0:
                    kb.gx.wait(g, gx0 + t * 16)
                if t >= 2:
                    kb.dve.wait(g, dve0 + (t - 2) * NDV + 2)
                kb.gx.inc(g.dma_start(
                    oasm[:, (t % 2) * 1040:(t % 2) * 1040 + 1040].rearrange("p (s c) -> p s c", c=130),
                    bass.AP(owno, base * 130,
                            [[130, 128], [SLAB * 130, 8], [1, 130]])), 16)
        @blk.vector
        def _(v):
            for t in range(NT):
                kb.gx.wait(v, gx0 + (t + 1) * 16)
                if t > 0:
                    kb.dve.wait(v)
                ob = (t % 2) * 8 * 130
                for h in range(16):
                    col = ob + (h // 2) * 130 + (h % 2) * 65
                    ins = v.reciprocal(rcp[:, h:h + 1], oasm[:, col + 64:col + 65])
                kb.dve.inc(ins)
                kb.dve.wait(v)
                for h in range(16):
                    col = ob + (h // 2) * 130 + (h % 2) * 65
                    ins = v.tensor_scalar(osc[:, h * 64:(h + 1) * 64],
                                          oasm[:, col:col + 64],
                                          rcp[:, h:h + 1], None, ALU.mult)
                kb.dve.inc(ins)
                kb.pe.wait(v, pe0 + t * 2 + 1)
                kb.dve.inc(v.tensor_copy(ofm[:, :], pt[:, :]))
                kb.pe.wait(v, pe0 + t * 2 + 2)
                kb.dve.inc(v.tensor_tensor(
                    kb.x_res[:, t * D:(t + 1) * D],
                    kb.x_res[:, t * D:(t + 1) * D], po[:, :], ALU.add))
        @blk.tensor
        def _(t_):
            for t in range(NT):
                kb.dve.wait(t_, dve0 + t * NDV + 2)
                if t > 0:
                    kb.dve.wait(t_, dve0 + (t - 1) * NDV + 3)
                for hp in range(8):
                    ins = t_.transpose(pt[:, hp * 128:(hp + 1) * 128],
                                       osc[:, hp * 128:(hp + 1) * 128],
                                       kb.ident[:, :])
                kb.pe.inc(ins)
                kb.dve.wait(t_, dve0 + t * NDV + 3)
                if t > 0:
                    kb.dve.wait(t_, dve0 + (t - 1) * NDV + 4)
                for half in range(2):
                    for k in range(8):
                        ins = t_.matmul(
                            po[:, half * 512:half * 512 + 512],
                            ofm[:, k * 128:(k + 1) * 128],
                            woa[:, k * 1024 + half * 512:k * 1024 + half * 512 + 512],
                            start=(k == 0), stop=(k == 7))
                kb.pe.inc(ins)



def build_ffn(kb, di, tag):
    nc = kb.nc
    with contextlib.ExitStack() as pctx:
        lnT = kb.sb(pctx, f"{tag}_lnT", [128, 8 * TPC], BF16)
        ln_tm = kb.sb(pctx, f"{tag}_lntm", [128, 2 * D], BF16)
        gatew = kb.sb(pctx, f"{tag}_gatew", [128, NT * 16], F32)
        shctx = contextlib.ExitStack()
        wsh = kb.sb(shctx, f"{tag}_wsh", [128, 3 * 8192], BF16)
        b2sh = kb.sb(shctx, f"{tag}_b2sh", [1, D], BF16)
        shw_tgt = [0]
        with nc.Block(f"{tag}_shw") as blk:
            @blk.gpsimd
            def _(g):
                kb.gx.inc(g.dma_start(
                    wsh.ap().rearrange("p (xx n) -> p xx n", n=1024)[:, 0:8, :],
                    di["w1t"].ap()[0:D, :].rearrange("(k p) n -> p k n", p=128)), 16)
                kb.gx.inc(g.dma_start(
                    wsh.ap().rearrange("p (xx n) -> p xx n", n=1024)[:, 8:16, :],
                    di["w3t"].ap()[0:D, :].rearrange("(k p) n -> p k n", p=128)), 16)
                kb.gx.inc(g.dma_start(
                    wsh.ap().rearrange("p (xx n) -> p xx n", n=1024)[:, 16:24, :],
                    di["w2t"].ap()[0:FDFF, :].rearrange("(k p) n -> p k n", p=128)), 16)
                kb.gx.inc(g.dma_start(b2sh[:, :], di["b2rows"].ap()[0:1, :]), 16)
                shw_tgt[0] = kb.gx.v
        with contextlib.ExitStack() as qctx:
            pst = [kb.ps(qctx, f"{tag}_pst{i}", [128, 1024], BF16) for i in range(2)]
            emit_ln(kb, tag, ln_tm, lnT=lnT, pst=pst)
        with contextlib.ExitStack() as qctx:
            emit_gating(kb, qctx, di, tag, lnT, gatew)
        with contextlib.ExitStack() as qctx:
            hs = kb.sb(qctx, f"{tag}_hs_sh", [128, 1024], BF16)
            h_sb = kb.sb(qctx, f"{tag}_h_sh", [128, 2 * 8 * 512], BF16)
            ph = [kb.ps(qctx, f"{tag}_phs{i}", [128, 512], F32) for i in range(4)]
            py = [kb.ps(qctx, f"{tag}_pys{i}", [128, 1024], F32) for i in range(2)]
            emit_expert_core(kb, f"{tag}_sh", lnT, gatew, None, wsh, b2sh,
                             hs, h_sb, ph, py, dff=FDFF, bcol0=0,
                             wslot=(kb.gx, kb.gx.v))
        shctx.close()

        with contextlib.ExitStack() as qctx:
            wbuf = [kb.sb(qctx, f"{tag}_w{i}", [128, 3 * 4096], BF16) for i in range(2)]
            b2r = [kb.sb(qctx, f"{tag}_b2r{i}", [1, D], BF16) for i in range(2)]
            hs = kb.sb(qctx, f"{tag}_hs", [128, 1024], BF16)
            h_sb = kb.sb(qctx, f"{tag}_h", [128, 2 * 4 * 512], BF16)
            ph = [kb.ps(qctx, f"{tag}_ph{i}", [128, 512], F32) for i in range(4)]
            py = [kb.ps(qctx, f"{tag}_py{i}", [128, 1024], F32) for i in range(2)]
            gslot = [kb.g0, kb.g1]

            def wload(g, e):
                i = e % 2
                sl = gslot[i]
                r0 = (1 + e) * D
                r2 = FDFF + e * DFF
                sl.inc(g.dma_start(
                    wbuf[i].ap().rearrange("p (xx n) -> p xx n", n=512)[:, 0:8, :],
                    di["w1t"].ap()[r0:r0 + D, 0:DFF].rearrange("(k p) n -> p k n", p=128)), 16)
                sl.inc(g.dma_start(
                    wbuf[i].ap().rearrange("p (xx n) -> p xx n", n=512)[:, 8:16, :],
                    di["w3t"].ap()[r0:r0 + D, 0:DFF].rearrange("(k p) n -> p k n", p=128)), 16)
                sl.inc(g.dma_start(
                    wbuf[i].ap().rearrange("p (xx n) -> p xx n", n=1024)[:, 8:12, :],
                    di["w2t"].ap()[r2:r2 + DFF, :].rearrange("(k p) n -> p k n", p=128)), 16)
                sl.inc(g.dma_start(b2r[i][:, :], di["b2rows"].ap()[1 + e:2 + e, :]), 16)

            with nc.Block(f"{tag}_pre") as blk:
                @blk.gpsimd
                def _(g):
                    wload(g, 0)

            emit_experts_merged(kb, tag, lnT, gatew, wbuf, b2r, hs, h_sb, ph, py,
                                gslot, wload)


def emit_experts_merged(kb, tag, lnT, gatew, wbuf, b2r, hs, h_sb, ph, py,
                        gslot, wload):
    nc = kb.nc
    nm = DFF // 128
    w1o, w3o, w2o = 0, 8 * DFF, 16 * DFF
    pe0, dve0, act0 = kb.pe.v, kb.dve.v, kb.act.v
    x_res = kb.x_res
    sched = []
    for ch in range(5):
        if ch < 4:
            for m in range(nm):
                if ch > 0 and m == 2:
                    for tt in range(4):
                        sched.append(("y", ch - 1, tt))
                sched.append(("h", ch, m))
        else:
            for tt in range(4):
                sched.append(("y", 3, tt))
    pe_idx = {ev: i + 1 for i, ev in enumerate(sched)}
    PE_PER = len(sched)
    dve_idx = {}
    di_ = 0
    for ch in range(4):
        for m in range(nm):
            dve_idx[("c", ch, m)] = (di_ := di_ + 1)
        for tt in range(4):
            dve_idx[("e", ch, tt)] = (di_ := di_ + 1)
    DVE_PER = di_
    act_idx = {}
    ai_ = 0
    for ch in range(4):
        for m in range(nm):
            act_idx[(ch, m)] = (ai_ := ai_ + 1)
    ACT_PER = ai_
    wtgts = [gslot[e % 2].v + (0 if e < 1 else 64 * ((e - 1) // 2 + 1))
             for e in range(E)]
    w0 = [gslot[0].v, gslot[1].v]
    def wtgt(e):
        i = e % 2
        nloads = e // 2 + 1
        return w0[i] + (nloads - (1 if i == 0 else 0)) * 64 + (0 if i == 1 else 0) \
            if False else w0[i] + (e // 2 + (1 if i == 0 else 1)) * 64 - 64

    planned = [None] * E
    planned[0] = w0[0]
    with nc.Block(f"{tag}_emerged") as blk:
        @blk.gpsimd
        def _(g):
            for e in range(1, E):
                if e >= 2:
                    g.wait_ge(kb.pe.h, pe0 + (e - 2) * PE_PER + PE_PER)
                wload(g, e)
                planned[e] = gslot[e % 2].v

        @blk.tensor
        def _(t_):
            for e in range(E):
                ebase_pe = pe0 + e * PE_PER
                ebase_dve = dve0 + e * DVE_PER
                ebase_act = act0 + e * ACT_PER
                gslot[e % 2].wait(t_, planned[e] if planned[e] is not None else w0[0])
                for ev in sched:
                    if ev[0] == "h":
                        _, ch, m = ev
                        gm = ch * nm + m
                        if gm >= 2:
                            pch, pm = (ch, m - 2) if m >= 2 else (ch - 1, nm - 2 + m)
                            kb.act.wait(t_, ebase_act + act_idx[(pch, pm)])
                            kb.dve.wait(t_, ebase_dve + dve_idx[("c", pch, pm)])
                        elif e > 0:
                            pch, pm = 3, nm - 2 + m
                            kb.act.wait(t_, ebase_act - ACT_PER + act_idx[(pch, pm)])
                            kb.dve.wait(t_, ebase_dve - DVE_PER + dve_idx[("c", pch, pm)])
                        tok0 = ch * 512
                        wb = wbuf[e % 2]
                        for k in range(8):
                            t_.matmul(ph[m % 2][:, :],
                                      wb[:, w1o + k * DFF + m * 128:w1o + k * DFF + (m + 1) * 128],
                                      lnT[:, k * TPC + tok0:k * TPC + tok0 + 512],
                                      start=(k == 0), stop=(k == 7))
                        for k in range(8):
                            ins = t_.matmul(ph[2 + m % 2][:, :],
                                            wb[:, w3o + k * DFF + m * 128:w3o + k * DFF + (m + 1) * 128],
                                            lnT[:, k * TPC + tok0:k * TPC + tok0 + 512],
                                            start=(k == 0), stop=(k == 7))
                        kb.pe.inc(ins)
                    else:
                        _, ch, tt = ev
                        gt = ch * 4 + tt
                        kb.dve.wait(t_, ebase_dve + dve_idx[("c", ch, nm - 1)])
                        if gt >= 2:
                            kb.dve.wait(t_, ebase_dve + dve_idx[("e", (gt - 2) // 4, (gt - 2) % 4)])
                        elif e > 0:
                            kb.dve.wait(t_, ebase_dve - DVE_PER + dve_idx[("e", 3, 2 + gt)])
                        pyb = py[tt % 2]
                        hoff = (ch % 2) * nm * 512
                        wb = wbuf[e % 2]
                        for hf in range(2):
                            for k in range(nm):
                                t_.matmul(pyb[:, hf * 512:hf * 512 + 512],
                                          h_sb[:, hoff + k * 512 + tt * 128:hoff + k * 512 + tt * 128 + 128],
                                          wb[:, w2o + k * 1024 + hf * 512:w2o + k * 1024 + hf * 512 + 512],
                                          start=(k == 0), stop=False)
                            ins = t_.matmul(pyb[:, hf * 512:hf * 512 + 512],
                                            kb.ones_row[:, :],
                                            b2r[e % 2][:, hf * 512:hf * 512 + 512],
                                            start=False, stop=True)
                        kb.pe.inc(ins)

        @blk.scalar
        def _(sc):
            for e in range(E):
                ebase_pe = pe0 + e * PE_PER
                ebase_dve = dve0 + e * DVE_PER
                for ch in range(4):
                    for m in range(nm):
                        kb.pe.wait(sc, ebase_pe + pe_idx[("h", ch, m)])
                        gm = ch * nm + m
                        if gm >= 2:
                            pch, pm = (ch, m - 2) if m >= 2 else (ch - 1, nm - 2 + m)
                            kb.act.wait(sc)
                            kb.dve.wait(sc, ebase_dve + dve_idx[("c", pch, pm)])
                        elif e > 0:
                            pch, pm = 3, nm - 2 + m
                            kb.act.wait(sc)
                            kb.dve.wait(sc, ebase_dve - DVE_PER + dve_idx[("c", pch, pm)])
                        kb.act.inc(sc.activation(hs[:, (m % 2) * 512:(m % 2) * 512 + 512],
                                                 ph[m % 2][:, :], AF.Silu,
                                                 bias=kb.b1cols[:, 8 + e * 4 + m:8 + e * 4 + m + 1]))

        @blk.vector
        def _(v):
            for e in range(E):
                ebase_pe = pe0 + e * PE_PER
                ebase_act = act0 + e * ACT_PER
                for ch in range(4):
                    if ch >= 2 or e > 0:
                        kb.dve.wait(v)
                        if ch >= 2:
                            kb.pe.wait(v, ebase_pe + pe_idx[("y", ch - 2, 3)])
                        else:
                            kb.pe.wait(v, ebase_pe - PE_PER + pe_idx[("y", ch + 2, 3)])
                    for m in range(nm):
                        kb.act.wait(v, ebase_act + act_idx[(ch, m)])
                        kb.dve.inc(v.scalar_tensor_tensor(
                            h_sb[:, (ch % 2) * nm * 512 + m * 512:(ch % 2) * nm * 512 + (m + 1) * 512],
                            ph[2 + m % 2][:, :],
                            kb.b3cols[:, 8 + e * 4 + m:8 + e * 4 + m + 1],
                            hs[:, (m % 2) * 512:(m % 2) * 512 + 512],
                            ALU.add, ALU.mult))
                    for tt in range(4):
                        gt = ch * 4 + tt
                        kb.pe.wait(v, ebase_pe + pe_idx[("y", ch, tt)])
                        pyb = py[tt % 2]
                        kb.dve.inc(v.scalar_tensor_tensor(
                            x_res[:, gt * D:(gt + 1) * D],
                            pyb[:, :],
                            gatew[:, gt * 16 + e:gt * 16 + e + 1],
                            x_res[:, gt * D:(gt + 1) * D],
                            ALU.mult, ALU.add))


def emit_gating(kb, qctx, di, tag, lnT, gatew):
    nc = kb.nc
    gwt = kb.sb(qctx, f"{tag}_gwt", [128, 8 * 16], BF16)
    ebr = kb.sb(qctx, f"{tag}_ebr", [1, 16], BF16)
    ebb = kb.sb(qctx, f"{tag}_ebb", [128, 16], F32)
    logit = kb.sb(qctx, f"{tag}_logit", [128, 16], F32)
    gsc = kb.sb(qctx, f"{tag}_gsc", [128, 80], F32)
    pg = kb.ps(qctx, f"{tag}_pg", [128, 16], F32)

    with nc.Block(f"{tag}_gw") as blk:
        @blk.gpsimd
        def _(g):
            kb.gx.inc(g.dma_start(
                gwt.ap().rearrange("p (k n) -> p k n", k=8),
                di["gate_wt"].ap().rearrange("(k p) n -> p k n", p=128)), 16)
            kb.gx.inc(g.dma_start(ebr[:, :], di["ebrow"].ap()), 16)
            kb.gx.wait(g)
        @blk.tensor
        def _(t_):
            kb.gx.wait(t_)
            kb.pe.inc(t_.matmul(pg[:, :], kb.ones_row[:, :], ebr[:, :],
                                start=True, stop=True))
        @blk.vector
        def _(v):
            kb.pe.wait(v)
            kb.dve.inc(v.tensor_copy(ebb[:, :], pg[:, :]))

    pe0, dve0, act0 = kb.pe.v, kb.dve.v, kb.act.v
    NDV = 17
    with nc.Block(f"{tag}_gate") as blk:
        @blk.tensor
        def _(t_):
            for t in range(NT):
                if t > 0:
                    kb.dve.wait(t_, dve0 + (t - 1) * NDV + 1)
                for k in range(8):
                    t_.matmul(pg[:, :], lnT[:, k * TPC + t * 128:k * TPC + (t + 1) * 128],
                              gwt[:, k * 16:(k + 1) * 16],
                              start=(k == 0), stop=False)
                kb.pe.inc(t_.matmul(pg[:, :], kb.ones_row[:, :], ebr[:, :],
                                    start=False, stop=True))
        @blk.vector
        def _(v):
            for t in range(NT):
                kb.pe.wait(v, pe0 + t + 1)
                if t > 0:
                    kb.dve.wait(v)
                    kb.act.wait(v, act0 + t)
                kb.dve.inc(v.tensor_copy(gsc[:, 0:16], pg[:, :]))
                kb.dve.wait(v)
                kb.dve.inc(v.tensor_tensor(logit[:, :], gsc[:, 0:16], ebb[:, :],
                                           ALU.subtract))
                kb.dve.inc(v.reduce_max(gsc[:, 64:65], gsc[:, 0:16], axis=AX))
                kb.dve.wait(v)
                kb.dve.inc(v.tensor_scalar(gsc[:, 16:32], gsc[:, 0:16],
                                           gsc[:, 64:65], 0.0, ALU.is_equal, ALU.add))
                kb.dve.wait(v)
                kb.dve.inc(v.scalar_tensor_tensor(gsc[:, 48:64], gsc[:, 16:32], -1e30,
                                                  gsc[:, 0:16], ALU.mult, ALU.add))
                kb.dve.wait(v)
                kb.dve.inc(v.reduce_max(gsc[:, 65:66], gsc[:, 48:64], axis=AX))
                kb.dve.wait(v)
                kb.dve.inc(v.tensor_scalar(gsc[:, 32:48], gsc[:, 48:64],
                                           gsc[:, 65:66], 0.0, ALU.is_equal, ALU.add))
                kb.dve.wait(v)
                kb.act.wait(v, act0 + t + 1)
                kb.dve.inc(v.scalar_tensor_tensor(gsc[:, 48:64], gsc[:, 16:32], 1.0,
                                                  logit[:, :], ALU.mult, ALU.mult,
                                                  accum_out=gsc[:, 66:67]))
                kb.dve.inc(v.scalar_tensor_tensor(kb.junkv[:, 0:16], gsc[:, 32:48], 1.0,
                                                  logit[:, :], ALU.mult, ALU.mult,
                                                  accum_out=gsc[:, 67:68]))
                kb.dve.inc(v.tensor_copy(kb.junkv[:, 16:32], gsc[:, 0:16]))
                kb.dve.wait(v)
                kb.dve.inc(v.tensor_tensor(gsc[:, 68:69], gsc[:, 66:67],
                                           gsc[:, 67:68], ALU.add))
                kb.dve.wait(v)
                kb.dve.inc(v.tensor_scalar(gsc[:, 68:69], gsc[:, 68:69], 1e-10,
                                           None, ALU.add))
                kb.dve.wait(v)
                kb.dve.inc(v.reciprocal(gsc[:, 69:70], gsc[:, 68:69]))
                kb.dve.wait(v)
                kb.dve.inc(v.tensor_tensor(gsc[:, 66:67], gsc[:, 66:67],
                                           gsc[:, 69:70], ALU.mult))
                kb.dve.inc(v.tensor_tensor(gsc[:, 67:68], gsc[:, 67:68],
                                           gsc[:, 69:70], ALU.mult))
                kb.dve.wait(v)
                kb.dve.inc(v.tensor_scalar(gsc[:, 48:64], gsc[:, 16:32],
                                           gsc[:, 66:67], None, ALU.mult))
                kb.dve.wait(v)
                kb.dve.inc(v.scalar_tensor_tensor(
                    gatew[:, t * 16:(t + 1) * 16], gsc[:, 32:48],
                    gsc[:, 67:68], gsc[:, 48:64], ALU.mult, ALU.add))
        @blk.scalar
        def _(sc):
            for t in range(NT):
                kb.dve.wait(sc, dve0 + t * NDV + 2)
                if t > 0:
                    kb.act.wait(sc)
                kb.act.inc(sc.activation(logit[:, :], logit[:, :], AF.Sigmoid))


def emit_expert_core(kb, name, lnT, gatew, gatew_col, wbuf_, b2r_, hs, h_sb,
                     ph, py, dff, bcol0, wslot=None, prefetch=None):
    nc = kb.nc
    nm = dff // 128
    w1o, w3o, w2o = 0, 8 * dff, 16 * dff
    pe0, dve0, act0 = kb.pe.v, kb.dve.v, kb.act.v
    x_res = kb.x_res
    def pe_h(ch, m):
        base = ch * (nm + 4)
        return pe0 + base + (m + 1 if m < 2 else m + 5 if ch > 0 else m + 1)
    def pe_y(ch, tt):
        return pe0 + (ch + 1) * (nm + 4) - (nm - 2) - 4 + tt + 1 if False else                pe0 + (ch + 1) * (nm + 4) + 2 + tt + 1 - (nm + 4) + nm - 2 + 0
    sched = []
    for ch in range(5):
        if ch < 4:
            for m in range(nm):
                if ch > 0 and m == 2:
                    for tt in range(4):
                        sched.append(("y", ch - 1, tt))
                sched.append(("h", ch, m))
        else:
            for tt in range(4):
                sched.append(("y", 3, tt))
    pe_idx = {ev: i + 1 for i, ev in enumerate(sched)}
    act_idx = {}
    dve_idx = {}
    di_ = 0
    for ch in range(4):
        for m in range(nm):
            dve_idx[("c", ch, m)] = (di_ := di_ + 1)
        for tt in range(4):
            dve_idx[("e", ch, tt)] = (di_ := di_ + 1)
    ai_ = 0
    for ch in range(4):
        for m in range(nm):
            act_idx[(ch, m)] = (ai_ := ai_ + 1)

    with nc.Block(name) as blk:
        if prefetch is not None:
            @blk.gpsimd
            def _(g):
                prefetch(g)
        @blk.tensor
        def _(t_):
            if wslot is not None:
                wslot[0].wait(t_, wslot[1])
            for ev in sched:
                if ev[0] == "h":
                    _, ch, m = ev
                    gm = ch * nm + m
                    if gm >= 2:
                        pch, pm = (ch, m - 2) if m >= 2 else (ch - 1, nm - 2 + m)
                        kb.act.wait(t_, act0 + act_idx[(pch, pm)])
                        kb.dve.wait(t_, dve0 + dve_idx[("c", pch, pm)])
                    tok0 = ch * 512
                    for k in range(8):
                        t_.matmul(ph[m % 2][:, :],
                                  wbuf_[:, w1o + k * dff + m * 128:w1o + k * dff + (m + 1) * 128],
                                  lnT[:, k * TPC + tok0:k * TPC + tok0 + 512],
                                  start=(k == 0), stop=(k == 7))
                    for k in range(8):
                        ins = t_.matmul(ph[2 + m % 2][:, :],
                                        wbuf_[:, w3o + k * dff + m * 128:w3o + k * dff + (m + 1) * 128],
                                        lnT[:, k * TPC + tok0:k * TPC + tok0 + 512],
                                        start=(k == 0), stop=(k == 7))
                    kb.pe.inc(ins)
                else:
                    _, ch, tt = ev
                    gt = ch * 4 + tt
                    kb.dve.wait(t_, dve0 + dve_idx[("c", ch, nm - 1)])
                    if gt >= 2:
                        kb.dve.wait(t_, dve0 + dve_idx[("e", (gt - 2) // 4, (gt - 2) % 4)])
                    pyb = py[tt % 2]
                    hoff = (ch % 2) * nm * 512
                    for hf in range(2):
                        for k in range(nm):
                            t_.matmul(pyb[:, hf * 512:hf * 512 + 512],
                                      h_sb[:, hoff + k * 512 + tt * 128:hoff + k * 512 + tt * 128 + 128],
                                      wbuf_[:, w2o + k * 1024 + hf * 512:w2o + k * 1024 + hf * 512 + 512],
                                      start=(k == 0), stop=False)
                        ins = t_.matmul(pyb[:, hf * 512:hf * 512 + 512],
                                        kb.ones_row[:, :], b2r_[:, hf * 512:hf * 512 + 512],
                                        start=False, stop=True)
                    kb.pe.inc(ins)

        @blk.scalar
        def _(sc):
            for ch in range(4):
                for m in range(nm):
                    kb.pe.wait(sc, pe0 + pe_idx[("h", ch, m)])
                    if ch * nm + m >= 2:
                        pch, pm = (ch, m - 2) if m >= 2 else (ch - 1, nm - 2 + m)
                        kb.act.wait(sc)
                        kb.dve.wait(sc, dve0 + dve_idx[("c", pch, pm)])
                    kb.act.inc(sc.activation(hs[:, (m % 2) * 512:(m % 2) * 512 + 512],
                                             ph[m % 2][:, :], AF.Silu,
                                             bias=kb.b1cols[:, bcol0 + m:bcol0 + m + 1]))

        @blk.vector
        def _(v):
            for ch in range(4):
                if ch >= 2:
                    kb.dve.wait(v)
                    kb.pe.wait(v, pe0 + pe_idx[("y", ch - 2, 3)])
                for m in range(nm):
                    kb.act.wait(v, act0 + act_idx[(ch, m)])
                    kb.dve.inc(v.scalar_tensor_tensor(
                        h_sb[:, (ch % 2) * nm * 512 + m * 512:(ch % 2) * nm * 512 + (m + 1) * 512],
                        ph[2 + m % 2][:, :],
                        kb.b3cols[:, bcol0 + m:bcol0 + m + 1],
                        hs[:, (m % 2) * 512:(m % 2) * 512 + 512],
                        ALU.add, ALU.mult))
                for tt in range(4):
                    gt = ch * 4 + tt
                    kb.pe.wait(v, pe0 + pe_idx[("y", ch, tt)])
                    pyb = py[tt % 2]
                    if gatew_col is None:
                        ins = v.tensor_tensor(
                            x_res[:, gt * D:(gt + 1) * D],
                            x_res[:, gt * D:(gt + 1) * D], pyb[:, :], ALU.add)
                    else:
                        ins = v.scalar_tensor_tensor(
                            x_res[:, gt * D:(gt + 1) * D],
                            pyb[:, :],
                            gatew[:, gt * 16 + gatew_col:gt * 16 + gatew_col + 1],
                            x_res[:, gt * D:(gt + 1) * D],
                            ALU.mult, ALU.add)
                    kb.dve.inc(ins)



def build_attn_b(kb, di, dscr):
    nc = kb.nc
    cc_in, cc_out = dscr["cc_gla_in"], dscr["cc_gla_out"]
    prefix_d = dscr["gla_prefix"]
    kp_d, el_d, o_d = dscr["gla_kp"], dscr["gla_el"], dscr["gla_o"]

    with contextlib.ExitStack() as pctx:
        lnT = kb.sb(pctx, "gb_lnT", [128, 8 * TPC], BF16)
        ln_tm = kb.sb(pctx, "gb_lntm", [128, 2 * D], BF16)
        with contextlib.ExitStack() as qctx:
            pst = [kb.ps(qctx, f"gb_pst{i}", [128, 1024], BF16) for i in range(2)]
            emit_ln(kb, "gb", ln_tm, lnT=lnT, pst=pst)

        v_tm = kb.sb(pctx, "gb_v", [128, NT * D], BF16)
        t1 = kb.sb(pctx, "gb_t1", [64, TPC], BF16)

        with contextlib.ExitStack() as qctx:
            wvb = kb.sb(qctx, "gb_wvb", [128, 8 * 1024], BF16)
            wg1s = kb.sb(qctx, "gb_wg1", [128, 8 * 64], BF16)
            pv = [kb.ps(qctx, f"gb_pv{i}", [128, 1024], F32) for i in range(2)]
            pg1 = kb.ps(qctx, "gb_pg1", [64, 2048], F32)
            with nc.Block("gb_a0w") as blk:
                @blk.gpsimd
                def _(g):
                    kb.gx.inc(g.dma_start(
                        wvb.ap().rearrange("p (k n) -> p k n", k=8),
                        di["wv_b"].ap().rearrange("(k p) n -> p k n", p=128)), 16)
                    kb.gx.inc(g.dma_start(
                        wg1s.ap().rearrange("p (k n) -> p k n", k=8),
                        di["wg1"].ap().rearrange("(k p) n -> p k n", p=128)), 16)
                    kb.gx.wait(g)
            pe0, dve0 = kb.pe.v, kb.dve.v
            with nc.Block("gb_a0") as blk:
                @blk.tensor
                def _(t_, pe0=pe0, dve0=dve0):
                    for t in range(NT):
                        if t >= 2:
                            kb.dve.wait(t_, dve0 + t - 1)
                        for hf in range(2):
                            for k in range(8):
                                ins = t_.matmul(pv[t % 2][:, hf * 512:hf * 512 + 512],
                                                lnT[:, k * TPC + t * 128:k * TPC + (t + 1) * 128],
                                                wvb[:, k * 1024 + hf * 512:k * 1024 + hf * 512 + 512],
                                                start=(k == 0), stop=(k == 7))
                        kb.pe.inc(ins)
                    for q4 in range(4):
                        for k in range(8):
                            ins = t_.matmul(pg1[:, q4 * 512:q4 * 512 + 512],
                                            wg1s[:, k * 64:(k + 1) * 64],
                                            lnT[:, k * TPC + q4 * 512:k * TPC + q4 * 512 + 512],
                                            start=(k == 0), stop=(k == 7))
                    kb.pe.inc(ins)
                @blk.vector
                def _(v, pe0=pe0):
                    for t in range(NT):
                        kb.pe.wait(v, pe0 + t + 1)
                        kb.dve.inc(v.tensor_copy(v_tm[:, t * D:(t + 1) * D],
                                                 pv[t % 2][:, :]))
                    kb.pe.wait(v, pe0 + NT + 1)
                    kb.dve.inc(v.tensor_copy(t1[:, :], pg1[:, :]))

        with contextlib.ExitStack() as qctx:
            wkh = kb.sb(qctx, "gb_wkh", [128, 2 * 8 * 128], BF16)
            csh = kb.sb(qctx, "gb_csh", [128, 2 * TPC], BF16)
            ekh = kb.sb(qctx, "gb_ekh", [128, TPC], F32)
            elh = kb.sb(qctx, "gb_elh", [128, TPC], F32)
            kph = kb.sb(qctx, "gb_kph", [128, TPC], BF16)
            ktm = kb.sb(qctx, "gb_ktm", [128, 128], BF16)
            stage = kb.sb(qctx, "gb_stage", [128, 2 * H * 65], F32)
            pkh = kb.ps(qctx, "gb_pkh", [128, 2048], F32)
            pstate = [kb.ps(qctx, f"gb_pstt{i}", [128, 64], F32) for i in range(2)]
            ptr = [kb.ps(qctx, f"gb_ptr{i}", [128, 128], BF16) for i in range(2)]

            with nc.Block("gb_a1pre") as blk:
                @blk.gpsimd
                def _(g):
                    kb.g0.inc(g.dma_start(
                        wkh.ap().rearrange("p (sx x) -> p sx x", sx=2)[:, 0, :].rearrange(
                            "p (k n) -> p k n", k=8),
                        di["wk_b_dup"].ap()[:, 0:128].rearrange("(k p) n -> p k n", p=128)), 16)
                    kb.g0.inc(g.dma_start(csh[:, 0:TPC], di["ropecs"].ap()[0]), 16)
                    kb.g1.inc(g.dma_start(
                        wkh.ap().rearrange("p (sx x) -> p sx x", sx=2)[:, 1, :].rearrange(
                            "p (k n) -> p k n", k=8),
                        di["wk_b_dup"].ap()[:, 128:256].rearrange("(k p) n -> p k n", p=128)), 16)
                    kb.g1.inc(g.dma_start(csh[:, TPC:2 * TPC], di["ropecs"].ap()[1]), 16)

            emit_gla_a1_merged(kb, di, lnT, v_tm, wkh, csh, ekh, elh, kph, ktm,
                               stage, pkh, pstate, ptr, kp_d, el_d)

            with nc.Block("gb_ship") as blk:
                @blk.sync
                def _(s):
                    kb.sx.inc(s.dma_start(
                        cc_in.ap().rearrange("(xx p) c -> p xx c", p=128),
                        stage.ap().rearrange("p (xx c) -> p xx c", c=65)), 16)
                    kb.sx.wait(s)
                @blk.gpsimd
                def _(g):
                    kb.sx.wait(g)
                    kb.cc.inc(g.collective_compute(
                        "AllGather", ALU.bypass, replica_groups=[list(range(8))],
                        ins=[cc_in.ap().opt()], outs=[cc_out.ap().opt()]))
                    kb.cc.wait(g)

        sin_sb = kb.sb(pctx, "gb_sin", [128, 2 * 1040], F32)
        einp1 = kb.sb(pctx, "gb_einp1", [128, 32], F32)
        with contextlib.ExitStack() as qctx:
            acc = kb.sb(qctx, "gb_acc", [128, 1040], F32)
            sts = kb.sb(qctx, "gb_sts", [128, 1040], F32)
            emit_gla_prefix(kb, cc_out, prefix_d, acc, sts, sin_sb, einp1)

        with contextlib.ExitStack() as qctx:
            wqh = kb.sb(qctx, "gb_wqh", [128, 2 * 8 * 128], BF16)
            csh2 = kb.sb(qctx, "gb_csh2", [128, 2 * TPC], BF16)
            kph2 = kb.sb(qctx, "gb_kph2", [128, 2 * TPC], BF16)
            elh2 = kb.sb(qctx, "gb_elh2", [128, 2 * TPC], BF16)
            sqh = kb.sb(qctx, "gb_sqh", [128, TPC], BF16)
            invh = kb.sb(qctx, "gb_invh", [128, TPC], F32)
            qph = kb.sb(qctx, "gb_qph", [128, TPC], BF16)
            ktm2 = kb.sb(qctx, "gb_ktm2", [128, 128], BF16)
            psb = kb.sb(qctx, "gb_psb", [128, 128], BF16)
            srun = kb.sb(qctx, "gb_srun", [128, 128], F32)
            srbf = kb.sb(qctx, "gb_srbf", [128, 128], BF16)
            osh = kb.sb(qctx, "gb_osh", [128, NT * 64], BF16)
            pqh = kb.ps(qctx, "gb_pqh", [128, 1024], F32)
            ptr2 = [kb.ps(qctx, f"gb_ptr2{i}", [128, 128], BF16) for i in range(2)]
            psc = [kb.ps(qctx, f"gb_psc{i}", [128, 128], F32) for i in range(2)]
            pou = [kb.ps(qctx, f"gb_pou{i}", [128, 128], F32) for i in range(2)]
            emit_gla_b_merged(kb, di, lnT, v_tm, sin_sb, einp1,
                              wqh, csh2, kph2, elh2, sqh, invh, qph, ktm2, psb,
                              srun, srbf, osh, pqh, ptr2, psc, pou,
                              kp_d, el_d, o_d)

        with contextlib.ExitStack() as qctx:
            emit_gla_c(kb, qctx, di, t1, o_d)


def emit_gla_a1_merged(kb, di, lnT, v_tm, wkh, csh, ekh, elh, kph, ktm, stage,
                       pkh, pstate, ptr, kp_d, el_d):
    nc = kb.nc
    pe0, dve0, act0, gx0 = kb.pe.v, kb.dve.v, kb.act.v, kb.gx.v
    g0v, g1v = kb.g0.v, kb.g1.v
    DVE_P = 21

    def wtgt(h):
        return (g0v if h % 2 == 0 else g1v) + 32 * (h // 2)

    def kpv(h):
        return pe0 + (h + 1 if h < 2 else 33 * h - 31)

    def S(h):
        return pe0 + (2 + 33 * h if h <= 14 else 496)

    def chT(h, gt):
        return S(h) + 1 + 2 * gt

    def chS(h, gt):
        return S(h) + 2 + 2 * gt

    views = {}
    for h in range(H):
        i = h % 2
        views[h] = dict(
            ws=wkh.ap().rearrange("p (sx x) -> p sx x", sx=2)[:, i, :],
            cs=csh.ap().rearrange("p (sx x) -> p sx x", sx=2)[:, i, :])

    with nc.Block("gb_a1") as blk:
        @blk.tensor
        def _(t_):
            def emit_kp(h):
                vw = views[h]
                gsl = kb.g0 if h % 2 == 0 else kb.g1
                gsl.wait(t_, wtgt(h))
                if h >= 1:
                    kb.act.wait(t_, act0 + h)
                ins = None
                for q4 in range(4):
                    for k in range(8):
                        ins = t_.matmul(pkh[:, q4 * 512:q4 * 512 + 512],
                                        vw["ws"][:, k * 128:(k + 1) * 128],
                                        lnT[:, k * TPC + q4 * 512:k * TPC + q4 * 512 + 512],
                                        start=(k == 0), stop=(k == 7))
                kb.pe.inc(ins)

            def emit_ch(h):
                dvb = dve0 + h * DVE_P
                for gt in range(NT):
                    sl, cchunk = gt // 8, gt % 8
                    kb.dve.wait(t_, dvb + 3 + max(0, gt - 1))
                    kb.pe.inc(t_.transpose(ptr[gt % 2][:, :],
                                           kph[:, gt * 128:(gt + 1) * 128],
                                           kb.ident[:, :]))
                    kb.dve.wait(t_, dvb + 4 + gt)
                    kb.pe.inc(t_.matmul(pstate[sl][:, :], ktm[:, :],
                                        v_tm[:, gt * D + h * 64:gt * D + h * 64 + 64],
                                        start=(cchunk == 0), stop=(cchunk == 7)))

            emit_kp(0)
            emit_kp(1)
            for h in range(H):
                emit_ch(h)
                if h + 2 < H:
                    emit_kp(h + 2)

        @blk.scalar
        def _(sc):
            for h in range(H):
                kb.pe.wait(sc, kpv(h))
                if h >= 1:
                    kb.dve.wait(sc, dve0 + (h - 1) * DVE_P + 3)
                kb.act.inc(sc.activation(ekh[:, :], pkh[:, :], AF.Exp))

        @blk.vector
        def _(v):
            for h in range(H):
                vw = views[h]
                dvb = dve0 + h * DVE_P
                kb.gx.wait(v, gx0 + 32 * h)
                kb.act.wait(v, act0 + h + 1)
                if h >= 1:
                    kb.dve.wait(v)
                kb.dve.inc(v.tensor_tensor_scan(elh[:, 0:SLAB], kb.ones_f[:, :],
                                                ekh[:, 0:SLAB], 0.0, ALU.mult, ALU.add))
                kb.dve.inc(v.tensor_tensor_scan(elh[:, SLAB:TPC], kb.ones_f[:, :],
                                                ekh[:, SLAB:TPC], 0.0, ALU.mult, ALU.add))
                if h >= 1:
                    kb.pe.wait(v, chT(h - 1, NT - 1))
                kb.dve.inc(v.tensor_tensor(kph[:, :], ekh[:, :], vw["cs"], ALU.mult))
                for gt in range(NT):
                    kb.pe.wait(v, chT(h, gt))
                    kb.dve.inc(v.tensor_copy(ktm[:, :], ptr[gt % 2][:, :]))
                kb.pe.wait(v, chS(h, NT - 1))
                for sl in range(2):
                    v.tensor_copy(stage[:, (sl * H + h) * 65:(sl * H + h) * 65 + 64],
                                  pstate[sl][:, :])
                    kb.dve.inc(v.tensor_copy(
                        stage[:, (sl * H + h) * 65 + 64:(sl * H + h) * 65 + 65],
                        elh[:, (sl + 1) * SLAB - 1:(sl + 1) * SLAB]))

        @blk.gpsimd
        def _(g):
            for h in range(H):
                dvb = dve0 + h * DVE_P
                kb.dve.wait(g, dvb + 3)
                kb.gx.inc(g.dma_start(kp_d.ap()[h * 128:(h + 1) * 128, :], kph[:, :]), 16)
                kb.gx.inc(g.dma_start(el_d.ap()[h * 128:(h + 1) * 128, :], elh[:, :]), 16)
                if h + 2 < H:
                    hn = h + 2
                    vw = views[hn]
                    gsl = kb.g0 if hn % 2 == 0 else kb.g1
                    kb.pe.wait(g, kpv(h))
                    gsl.inc(g.dma_start(
                        vw["ws"].rearrange("p (k n) -> p k n", k=8),
                        di["wk_b_dup"].ap()[:, hn * 128:(hn + 1) * 128].rearrange(
                            "(k p) n -> p k n", p=128)), 16)
                    gsl.inc(g.dma_start(vw["cs"], di["ropecs"].ap()[hn]), 16)


def emit_gla_prefix(kb, cc_out, prefix_d, acc, sts, sin_sb, einp1):
    nc = kb.nc
    dve0, gx0 = kb.dve.v, kb.gx.v
    with nc.Block("gb_prefix") as blk:
        @blk.gpsimd
        def _(g):
            if True:
                ro = kb.ro
                for s_idx in range(NSLAB):
                    kb.dve.wait(g, dve0 + 1 + s_idx)
                    kb.gx.inc(g.dma_start(
                        prefix_d.ap()[s_idx * 128:(s_idx + 1) * 128, :], acc[:, :]), 16)
                    g.reg_load(ro, kb.offs[0:1, 16 + s_idx:17 + s_idx])
                    kb.gx.inc(g.dma_start(
                        sts.ap().rearrange("p (hh c) -> p hh c", c=65),
                        bass.AP(cc_out, ro, [[65, 128], [65 * 128, 16], [1, 65]])), 16)
                kb.dve.wait(g, dve0 + 1 + NSLAB)
                for sl in range(2):
                    g.reg_load(ro, kb.offs[0:1, 24 + sl:25 + sl])
                    kb.gx.inc(g.dma_start(
                        sin_sb[:, sl * 1040:(sl + 1) * 1040],
                        bass.AP(prefix_d, ro, [[1040, 128], [1, 1040]])), 16)
        @blk.vector
        def _(v):
            kb.dve.inc(v.memset(acc[:, :], 0.0))
            for s_idx in range(NSLAB):
                kb.gx.wait(v, gx0 + 16 * (2 * s_idx + 2))
                kb.dve.inc(v.tensor_tensor(acc[:, :], acc[:, :], sts[:, :], ALU.add))
            kb.gx.wait(v, gx0 + 16 * (2 * NSLAB + 2))
            for sl in range(2):
                sview = sin_sb.ap().rearrange("p (sx x) -> p sx x", sx=2)[
                    :, sl, :].rearrange("p (hh c) -> p hh c", c=65)[:, :, 64:65]
                kb.dve.inc(v.tensor_scalar(
                    einp1[:, sl * 16:sl * 16 + 16],
                    sview.rearrange("p hh c -> p (hh c)"),
                    1.0, None, ALU.add))


def emit_gla_b_merged(kb, di, lnT, v_tm, sin_sb, einp1, wqh, csh, kph, elh,
                      sqh, invh, qph, ktm, psb, srun, srbf, osh, pqh, ptr, psc,
                      pou, kp_d, el_d, o_d):
    nc = kb.nc
    pe0, dve0, act0, gx0 = kb.pe.v, kb.dve.v, kb.act.v, kb.gx.v
    PE_P, ACT_P, DVE_P = 50, 6, 84
    gsl_t = {}
    gsl_t[0] = kb.g0.v + 64
    gsl_t[1] = kb.g1.v + 64
    views = {}
    for h in range(H):
        i = h % 2
        views[h] = dict(
            ws=wqh.ap().rearrange("p (sx x) -> p sx x", sx=2)[:, i, :],
            cs=csh.ap().rearrange("p (sx x) -> p sx x", sx=2)[:, i, :],
            kpv=kph.ap().rearrange("p (sx x) -> p sx x", sx=2)[:, i, :],
            elv=elh.ap().rearrange("p (sx x) -> p sx x", sx=2)[:, i, :])

    with nc.Block("gb_b") as blk:
        @blk.gpsimd
        def _(g):
            for h in range(2):
                vw = views[h]
                gsl = kb.g0 if h % 2 == 0 else kb.g1
                gsl.inc(g.dma_start(
                    vw["ws"].rearrange("p (k n) -> p k n", k=8),
                    di["wq_b_dup"].ap()[:, h * 128:(h + 1) * 128].rearrange(
                        "(k p) n -> p k n", p=128)), 16)
                gsl.inc(g.dma_start(vw["cs"], di["ropecs"].ap()[h]), 16)
                gsl.inc(g.dma_start(vw["kpv"], kp_d.ap()[h * 128:(h + 1) * 128, :]), 16)
                gsl.inc(g.dma_start(vw["elv"], el_d.ap()[h * 128:(h + 1) * 128, :]), 16)
            for h in range(H):
                kb.dve.wait(g, dve0 + (h + 1) * DVE_P)
                kb.gx.inc(g.dma_start(
                    o_d.ap().rearrange("(t p) c -> p t c", p=128)[:, :, h * 64:(h + 1) * 64],
                    osh.ap().rearrange("p (t c) -> p t c", c=64)), 16)
                if h + 2 < H:
                    hn = h + 2
                    vw = views[hn]
                    gsl = kb.g0 if hn % 2 == 0 else kb.g1
                    kb.pe.wait(g, pe0 + (h + 1) * PE_P)
                    gsl.inc(g.dma_start(
                        vw["ws"].rearrange("p (k n) -> p k n", k=8),
                        di["wq_b_dup"].ap()[:, hn * 128:(hn + 1) * 128].rearrange(
                            "(k p) n -> p k n", p=128)), 16)
                    gsl.inc(g.dma_start(vw["cs"], di["ropecs"].ap()[hn]), 16)
                    gsl.inc(g.dma_start(vw["kpv"], kp_d.ap()[hn * 128:(hn + 1) * 128, :]), 16)
                    gsl.inc(g.dma_start(vw["elv"], el_d.ap()[hn * 128:(hn + 1) * 128, :]), 16)
                    gsl_t[hn] = gsl.v

        @blk.tensor
        def _(t_):
            for h in range(H):
                vw = views[h]
                pe_b = pe0 + h * PE_P
                dve_b = dve0 + h * DVE_P
                act_b = act0 + h * ACT_P
                gsl = kb.g0 if h % 2 == 0 else kb.g1
                gsl.wait(t_, gsl_t[h])
                if h > 0:
                    kb.act.wait(t_, act_b - ACT_P + 2)
                ins = None
                for hq in range(2):
                    if hq == 1:
                        kb.act.wait(t_, act_b + 1)
                    for q4 in range(2):
                        for k in range(8):
                            ins = t_.matmul(pqh[:, q4 * 512:q4 * 512 + 512],
                                            vw["ws"][:, k * 128:(k + 1) * 128],
                                            lnT[:, k * TPC + hq * 1024 + q4 * 512:k * TPC + hq * 1024 + q4 * 512 + 512],
                                            start=(k == 0), stop=(k == 7))
                    kb.pe.inc(ins)
                for i, (sl, cchunk) in enumerate(ORDER_B):
                    gt = sl * 8 + cchunk
                    if i >= 2:
                        kb.dve.wait(t_, dve_b + 4 + 5 * (i - 2) + 1)
                    elif h > 0:
                        kb.dve.wait(t_, dve_b - DVE_P + 4 + 5 * (14 + i) + 1)
                    kb.pe.inc(t_.transpose(ptr[i % 2][:, :], vw["kpv"][:, gt * 128:(gt + 1) * 128],
                                           kb.ident[:, :]))
                    kb.dve.wait(t_, dve_b + 2)
                    if i >= 2:
                        kb.dve.wait(t_, dve_b + 4 + 5 * (i - 2) + 2)
                    elif h > 0:
                        kb.dve.wait(t_, dve_b - DVE_P + 4 + 5 * (14 + i) + 2)
                    kb.pe.inc(t_.matmul(psc[i % 2][:, :], vw["kpv"][:, gt * 128:(gt + 1) * 128],
                                        qph[:, gt * 128:(gt + 1) * 128], start=True, stop=True))
                    kb.dve.wait(t_, dve_b + 4 + 5 * i + 2)
                    t_.matmul(pou[i % 2][:, 0:64], psb[:, :],
                              v_tm[:, gt * D + h * 64:gt * D + h * 64 + 64],
                              start=True, stop=False)
                    t_.matmul(pou[i % 2][:, 0:64], qph[:, gt * 128:(gt + 1) * 128],
                              srbf[:, sl * 64:sl * 64 + 64], start=False, stop=True)
                    kb.pe.inc(t_.matmul(pou[i % 2][:, 64:128], ktm[:, :],
                                        v_tm[:, gt * D + h * 64:gt * D + h * 64 + 64],
                                        start=True, stop=True))

        @blk.scalar
        def _(sc):
            for h in range(H):
                vw = views[h]
                pe_b = pe0 + h * PE_P
                dve_b = dve0 + h * DVE_P
                kb.pe.wait(sc, pe_b + 1)
                if h > 0:
                    kb.dve.wait(sc, dve_b - DVE_P + 2)
                    kb.act.wait(sc)
                kb.act.inc(sc.activation(sqh[:, 0:1024], pqh[:, :], AF.Silu))
                kb.pe.wait(sc, pe_b + 2)
                kb.act.inc(sc.activation(sqh[:, 1024:2048], pqh[:, :], AF.Silu))
                for sl in range(2):
                    kb.act.inc(sc.activation(invh[:, sl * SLAB:(sl + 1) * SLAB],
                                             vw["elv"][:, sl * SLAB:(sl + 1) * SLAB], AF.Ln,
                                             bias=einp1[:, sl * 16 + h:sl * 16 + h + 1]))
                    kb.act.wait(sc)
                    kb.act.inc(sc.activation(invh[:, sl * SLAB:(sl + 1) * SLAB],
                                             invh[:, sl * SLAB:(sl + 1) * SLAB], AF.Exp,
                                             scale=f32(-1.0)))

        @blk.vector
        def _(v):
            for h in range(H):
                vw = views[h]
                pe_b = pe0 + h * PE_P
                act_b = act0 + h * ACT_P
                kb.gx.wait(v, gx0 + h * 16)
                kb.act.wait(v, act_b + 6)
                if h > 0:
                    kb.dve.wait(v)
                kb.dve.inc(v.tensor_tensor(invh[:, :], invh[:, :], sqh[:, :], ALU.mult))
                kb.dve.wait(v)
                if h > 0:
                    kb.pe.wait(v, pe_b)
                kb.dve.inc(v.tensor_tensor(qph[:, :], invh[:, :], vw["cs"], ALU.mult))
                kb.dve.wait(v)
                for sl in range(2):
                    kb.dve.inc(v.tensor_copy(
                        srbf[:, sl * 64:sl * 64 + 64],
                        sin_sb[:, sl * 1040 + h * 65:sl * 1040 + h * 65 + 64]))
                    v.tensor_copy(srun[:, sl * 64:sl * 64 + 64],
                                  sin_sb[:, sl * 1040 + h * 65:sl * 1040 + h * 65 + 64])
                for i, (sl, cchunk) in enumerate(ORDER_B):
                    gt = sl * 8 + cchunk
                    kb.pe.wait(v, pe_b + 3 + 3 * i)
                    kb.dve.inc(v.tensor_copy(ktm[:, :], ptr[i % 2][:, :]))
                    kb.pe.wait(v, pe_b + 4 + 3 * i)
                    kb.dve.inc(v.tensor_tensor(psb[:, :], psc[i % 2][:, :],
                                               kb.trimask[:, :], ALU.mult))
                    kb.pe.wait(v, pe_b + 5 + 3 * i)
                    kb.dve.inc(v.tensor_tensor(srun[:, sl * 64:sl * 64 + 64],
                                               srun[:, sl * 64:sl * 64 + 64],
                                               pou[i % 2][:, 64:128], ALU.add))
                    kb.dve.wait(v)
                    kb.dve.inc(v.tensor_copy(srbf[:, sl * 64:sl * 64 + 64],
                                             srun[:, sl * 64:sl * 64 + 64]))
                    kb.dve.inc(v.tensor_copy(osh[:, gt * 64:(gt + 1) * 64],
                                             pou[i % 2][:, 0:64]))


def emit_gla_c(kb, qctx, di, t1, o_d):
    nc = kb.nc
    wg2s = kb.sb(qctx, "gb_wg2", [64, 1024], BF16)
    wob = kb.sb(qctx, "gb_wob", [128, 8 * 1024], BF16)
    ot = kb.sb(qctx, "gb_ot", [128, 2048], BF16)
    gg = kb.sb(qctx, "gb_gg", [128, 1024], BF16)
    gsig = kb.sb(qctx, "gb_gsig", [128, 1024], BF16)
    nrm = kb.sb(qctx, "gb_nrm", [128, 1024], BF16)
    nfm = kb.sb(qctx, "gb_nfm", [128, 1024], BF16)
    rst = kb.sb(qctx, "gb_rst", [128, 48], F32)
    pg2 = kb.ps(qctx, "gb_pg2", [128, 1024], F32)
    pnt = kb.ps(qctx, "gb_pnt", [128, 1024], BF16)
    pyo = kb.ps(qctx, "gb_pyo", [128, 1024], F32)

    with nc.Block("gb_cw") as blk:
        @blk.gpsimd
        def _(g):
            kb.gx.inc(g.dma_start(wg2s[:, :], di["wg2"].ap()), 16)
            kb.gx.inc(g.dma_start(
                wob.ap().rearrange("p (k n) -> p k n", k=8),
                di["wo_b"].ap().rearrange("(k p) n -> p k n", p=128)), 16)
            kb.gx.wait(g)
        @blk.vector
        def _(v):
            kb.gx.wait(v)
            for k in range(8):
                v.tensor_scalar(wob[:, k * 1024:(k + 1) * 1024],
                                wob[:, k * 1024:(k + 1) * 1024],
                                kb.nw[:, k:k + 1], None, ALU.mult)

    pe0, dve0, act0, gx0 = kb.pe.v, kb.dve.v, kb.act.v, kb.gx.v
    NDV = 21
    with nc.Block("gb_c") as blk:
        @blk.gpsimd
        def _(g):
            for t in range(NT):
                if t > 0:
                    kb.gx.wait(g, gx0 + t * 16)
                if t >= 2:
                    kb.dve.wait(g, dve0 + (t - 2) * NDV + 1)
                kb.gx.inc(g.dma_start(
                    ot[:, (t % 2) * 1024:(t % 2) * 1024 + 1024],
                    o_d.ap()[t * 128:(t + 1) * 128, :]), 16)
        @blk.tensor
        def _(t_):
            for t in range(NT):
                if t > 0:
                    kb.act.wait(t_, act0 + (t - 1) * 3 + 1)
                for hf in range(2):
                    ins = t_.matmul(pg2[:, hf * 512:hf * 512 + 512],
                                    t1[:, t * 128:(t + 1) * 128],
                                    wg2s[:, hf * 512:hf * 512 + 512],
                                    start=True, stop=True)
                kb.pe.inc(ins)
                kb.dve.wait(t_, dve0 + t * NDV + 19)
                if t > 0:
                    kb.dve.wait(t_, dve0 + (t - 1) * NDV + 20)
                for hp in range(8):
                    ins = t_.transpose(pnt[:, hp * 128:(hp + 1) * 128],
                                       nrm[:, hp * 128:(hp + 1) * 128],
                                       kb.ident[:, :])
                kb.pe.inc(ins)
                kb.dve.wait(t_, dve0 + t * NDV + 20)
                if t > 0:
                    kb.dve.wait(t_, dve0 + (t - 1) * NDV + 21)
                for hf in range(2):
                    for k in range(8):
                        ins = t_.matmul(
                            pyo[:, hf * 512:hf * 512 + 512],
                            nfm[:, k * 128:(k + 1) * 128],
                            wob[:, k * 1024 + hf * 512:k * 1024 + hf * 512 + 512],
                            start=(k == 0), stop=(k == 7))
                kb.pe.inc(ins)
        @blk.scalar
        def _(sc):
            for t in range(NT):
                kb.pe.wait(sc, pe0 + t * 3 + 1)
                if t > 0:
                    kb.dve.wait(sc, dve0 + (t - 1) * NDV + 1)
                    kb.act.wait(sc)
                kb.act.inc(sc.activation(gsig[:, :], pg2[:, :], AF.Sigmoid))
                kb.dve.wait(sc, dve0 + t * NDV + 18)
                kb.act.inc(sc.activation(rst[:, 16:32], rst[:, 0:16], AF.Ln,
                                         scale=f32(1.0 / HD), bias=f32(EPS_RMS)))
                kb.act.wait(sc)
                kb.act.inc(sc.activation(rst[:, 32:48], rst[:, 16:32], AF.Exp,
                                         scale=f32(-0.5)))
        @blk.vector
        def _(v):
            for t in range(NT):
                kb.act.wait(v, act0 + t * 3 + 1)
                kb.gx.wait(v, gx0 + (t + 1) * 16)
                if t > 0:
                    kb.dve.wait(v)
                kb.dve.inc(v.tensor_tensor(gg[:, :],
                                           ot[:, (t % 2) * 1024:(t % 2) * 1024 + 1024],
                                           gsig[:, :], ALU.mult))
                kb.dve.wait(v)
                for h in range(16):
                    kb.dve.inc(v.scalar_tensor_tensor(
                        kb.junk[:, h * 64:(h + 1) * 64], gg[:, h * 64:(h + 1) * 64], 1.0,
                        gg[:, h * 64:(h + 1) * 64], ALU.mult, ALU.mult,
                        accum_out=rst[:, h:h + 1]))
                kb.dve.inc(v.tensor_copy(kb.junkv[:, 0:16], gg[:, 0:16]))
                kb.act.wait(v, act0 + t * 3 + 3)
                for h in range(16):
                    ins = v.tensor_scalar(nrm[:, h * 64:(h + 1) * 64],
                                          gg[:, h * 64:(h + 1) * 64],
                                          rst[:, 32 + h:33 + h], None, ALU.mult)
                kb.dve.inc(ins)
                kb.pe.wait(v, pe0 + t * 3 + 2)
                kb.dve.inc(v.tensor_copy(nfm[:, :], pnt[:, :]))
                kb.pe.wait(v, pe0 + t * 3 + 3)
                kb.dve.inc(v.tensor_tensor(
                    kb.x_res[:, t * D:(t + 1) * D],
                    kb.x_res[:, t * D:(t + 1) * D], pyo[:, :], ALU.add))


def build_store(kb, t_y):
    nc = kb.nc
    with nc.Block("store") as blk:
        @blk.sync
        def _(s):
            kb.out.inc(s.dma_start(
                t_y.ap().rearrange("(t p) c -> p t c", p=128),
                kb.x_res.ap().rearrange("p (t c) -> p t c", c=D)), 16)
            kb.out.wait(s)



def kernel(**inputs):
    in_maps = prep_in_maps(inputs)
    nc = build_nc()
    res = run_bass_kernel_spmd(nc, in_maps, core_ids=list(range(8)))
    out = np.zeros((B, S, D), np.float32)
    for cix in range(8):
        seq, j = cix // 4, cix % 4
        y = res.results[cix]["y"]
        out[seq, j * SLAB:(j + 1) * SLAB] = y[:SLAB]
        out[seq, (7 - j) * SLAB:(8 - j) * SLAB] = y[SLAB:]
    return out


# version 07 (final)
# speedup vs baseline: 1.2533x; 1.0147x over previous
import contextlib
import numpy as np
import ml_dtypes
import concourse.bass as bass
import concourse.mybir as mybir
from concourse.bass_utils import run_bass_kernel_spmd

F32 = mybir.dt.float32
BF16 = mybir.dt.bfloat16
I32 = mybir.dt.int32
AF = mybir.ActivationFunctionType
ALU = mybir.AluOpType
AX = mybir.AxisListType.X
f32 = np.float32

B, S, D = 2, 8192, 1024
H, HD = 16, 64
NSLAB = 8
SLAB = S // NSLAB
TPC = 2 * SLAB
NT = TPC // 128
E, DFF, FDFF = 16, 512, 1024
EPS_LN = 1e-5
EPS_RMS = 1e-6
MSHIFT = 30.0
ROPE_BASE = 10000.0

PHASES = ("attn_a", "ffn1", "attn_b", "ffn2")
ORDER_B = [(sl, c) for c in range(8) for sl in range(2)]
AA_STAGES = 4
AA_NPB = 8
AA_PP = 15
DBG = ()


def _bf(x):
    return np.ascontiguousarray(x).astype(ml_dtypes.bfloat16)


def _owner(s_idx):
    j = min(s_idx, 7 - s_idx)
    return j, (0 if s_idx == j else 1)



def prep_in_maps(inputs):
    x = np.asarray(inputs["L0_attn_a_x_BSE"], np.float32)
    wq_a = np.asarray(inputs["L0_attn_a_w_q"], np.float32).reshape(D, H * HD)
    wk_a = np.asarray(inputs["L0_attn_a_w_k"], np.float32).reshape(D, H * HD)
    wv_a = np.asarray(inputs["L0_attn_a_w_v"], np.float32).reshape(D, H * HD)
    wo_a = np.ascontiguousarray(
        np.asarray(inputs["L0_attn_a_w_o"], np.float32).reshape(D, H * HD).T)

    wq_b = np.asarray(inputs["L0_attn_b_wq"], np.float32)
    wk_b = np.asarray(inputs["L0_attn_b_wk"], np.float32)
    wv_b = np.asarray(inputs["L0_attn_b_wv"], np.float32)
    wo_b = np.asarray(inputs["L0_attn_b_wo"], np.float32)
    wg1 = np.asarray(inputs["L0_attn_b_wg1"], np.float32)
    wg2 = np.asarray(inputs["L0_attn_b_wg2"], np.float32)
    norm_w = np.asarray(inputs["L0_attn_b_norm_w"], np.float32)

    gate_w = np.asarray(inputs["L0_ffn_gate_w"], np.float32)
    eb = np.asarray(inputs["L0_ffn_expert_bias"], np.float32)
    w1 = np.asarray(inputs["L0_ffn_w1"], np.float32)
    b1 = np.asarray(inputs["L0_ffn_b1"], np.float32)
    w2 = np.asarray(inputs["L0_ffn_w2"], np.float32)
    b2 = np.asarray(inputs["L0_ffn_b2"], np.float32)
    w3 = np.asarray(inputs["L0_ffn_w3"], np.float32)
    b3 = np.asarray(inputs["L0_ffn_b3"], np.float32)
    sw1 = np.asarray(inputs["L0_ffn_sw1"], np.float32)
    sb1 = np.asarray(inputs["L0_ffn_sb1"], np.float32)
    sw2 = np.asarray(inputs["L0_ffn_sw2"], np.float32)
    sb2 = np.asarray(inputs["L0_ffn_sb2"], np.float32)
    sw3 = np.asarray(inputs["L0_ffn_sw3"], np.float32)
    sb3 = np.asarray(inputs["L0_ffn_sb3"], np.float32)

    w1t = np.zeros(((E + 1) * D, FDFF), np.float32)
    w3t = np.zeros(((E + 1) * D, FDFF), np.float32)
    w2t = np.zeros((FDFF + E * DFF, D), np.float32)
    w1t[0:D, :] = sw1.T
    w3t[0:D, :] = sw3.T
    w2t[0:FDFF, :] = sw2.T
    for e in range(E):
        w1t[(1 + e) * D:(2 + e) * D, 0:DFF] = w1[e].T
        w3t[(1 + e) * D:(2 + e) * D, 0:DFF] = w3[e].T
        w2t[FDFF + e * DFF:FDFF + (e + 1) * DFF, :] = w2[e].T
    b1cols = np.zeros((128, 72), np.float32)
    b3cols = np.zeros((128, 72), np.float32)
    for m in range(8):
        b1cols[:, m] = sb1[m * 128:(m + 1) * 128]
        b3cols[:, m] = sb3[m * 128:(m + 1) * 128]
    for e in range(E):
        for m in range(4):
            b1cols[:, 8 + e * 4 + m] = b1[e, m * 128:(m + 1) * 128]
            b3cols[:, 8 + e * 4 + m] = b3[e, m * 128:(m + 1) * 128]
    b2rows = np.concatenate([sb2[None, :], b2], axis=0)
    gate_wt = np.ascontiguousarray(gate_w.T)
    ebrow = eb.reshape(1, E)

    def dup(w):
        wr = w.reshape(D, H, HD)
        return np.ascontiguousarray(
            np.concatenate([wr, wr], axis=2).reshape(D, H * 2 * HD))
    wq_b_dup = dup(wq_b)
    wk_b_dup = dup(wk_b)
    nw_cols = np.ascontiguousarray(norm_w.reshape(8, 128).T)

    dmask = np.zeros((4, 128, 512), np.float32)
    qq = np.arange(512)[None, :]
    kk = np.arange(128)[:, None]
    for d_ in range(4):
        dmask[d_] = (128 * d_ + kk <= qq)
    trimask = (np.arange(128)[:, None] <= np.arange(128)[None, :]).astype(np.float32)
    ident = np.eye(128, dtype=np.float32)

    shared = dict(
        w1t=w1t, w3t=w3t, w2t=w2t,
        b1cols=b1cols, b3cols=b3cols, b2rows=b2rows,
        gate_wt=gate_wt, ebrow=ebrow,
        wq_b_dup=wq_b_dup, wk_b_dup=wk_b_dup, wv_b=wv_b, wo_b=wo_b,
        wg1=wg1, wg2=wg2, nw_cols=nw_cols, wo_a=wo_a,
        dmask=_bf(dmask), trimask=_bf(trimask), ident=_bf(ident),
    )

    theta = (ROPE_BASE ** (-2.0 / (H * HD) * np.arange(H * HD, dtype=np.float64))
             ).reshape(H, HD)
    in_maps = []
    for c in range(8):
        seq, j = c // 4, c % 4
        slabs = (j, 7 - j)
        tok_rows = np.concatenate(
            [np.arange(sl * SLAB, (sl + 1) * SLAB) for sl in slabs])
        x_own = np.ascontiguousarray(x[seq, tok_rows, :])

        hsel = slice(2 * c * HD, (2 * c + 2) * HD)
        wq_a_sl = np.ascontiguousarray(wq_a[:, hsel])
        wk_a_sl = np.ascontiguousarray(wk_a[:, hsel])
        wv_a_sl = np.ascontiguousarray(wv_a[:, hsel])

        pos = tok_rows.astype(np.float64)
        ang = pos[None, None, :] * theta[:, :, None]
        ropecs = np.empty((H, 128, TPC), np.float64)
        ropecs[:, :HD] = np.cos(ang)
        ropecs[:, HD:] = np.sin(ang)
        ropecs = ropecs.astype(np.float32)

        oag_base = np.zeros((1, 2), np.int32)
        for i2, sl in enumerate(slabs):
            oag_base[0, i2] = (seq * S + sl * SLAB) * 130
        gla_state_off = np.zeros((1, 8), np.int32)
        rank_rows = 2 * H * 128
        for s_idx in range(NSLAB):
            j2, half = _owner(s_idx)
            gla_state_off[0, s_idx] = ((4 * seq + j2) * rank_rows + half * H * 128) * 65
        my_prefix_off = np.zeros((1, 2), np.int32)
        for i, sl in enumerate(slabs):
            my_prefix_off[0, i] = sl * 128 * 1040

        m = dict(shared)
        m.update(x_own=x_own, wq_a_sl=wq_a_sl, wk_a_sl=wk_a_sl, wv_a_sl=wv_a_sl,
                 ropecs=ropecs, oag_base=oag_base, gla_state_off=gla_state_off,
                 my_prefix_off=my_prefix_off)
        in_maps.append(m)
    return in_maps



class Sem:
    __slots__ = ("h", "v")

    def __init__(self, h):
        self.h = h
        self.v = 0

    def inc(self, ins, n=1):
        ins.then_inc(self.h, n)
        self.v += n
        return self.v

    def wait(self, eng, val=None):
        eng.wait_ge(self.h, self.v if val is None else val)


class Kb:
    def __init__(self, nc, ctx):
        self.nc = nc
        self.ctx = ctx

    def sem(self, name):
        return Sem(self.ctx.enter_context(self.nc.semaphore(name)))

    def sb(self, ctx, name, shape, dtype):
        return ctx.enter_context(self.nc.sbuf_tensor(name, shape, dtype))

    def ps(self, ctx, name, shape, dtype=F32):
        return ctx.enter_context(self.nc.psum_tensor(name, shape, dtype))


def build_nc(phases=None):
    phases = PHASES if phases is None else phases
    nc = bass.Bass("TRN2", target_bir_lowering=False)

    t_x = nc.dram_tensor("x_own", [TPC, D], F32, kind="ExternalInput")
    t_y = nc.dram_tensor("y", [TPC, D], F32, kind="ExternalOutput")
    di = {}
    for nm, shp, dt in [
        ("wq_a_sl", [D, 128], F32), ("wk_a_sl", [D, 128], F32),
        ("wv_a_sl", [D, 128], F32), ("wo_a", [D, D], F32),
        ("wq_b_dup", [D, 2048], F32), ("wk_b_dup", [D, 2048], F32),
        ("wv_b", [D, D], F32), ("wo_b", [D, D], F32),
        ("wg1", [D, 64], F32), ("wg2", [64, D], F32), ("nw_cols", [128, 8], F32),
        ("w1t", [(E + 1) * D, FDFF], F32), ("w3t", [(E + 1) * D, FDFF], F32),
        ("w2t", [FDFF + E * DFF, D], F32),
        ("b1cols", [128, 72], F32), ("b3cols", [128, 72], F32), ("b2rows", [17, D], F32),
        ("gate_wt", [D, 16], F32), ("ebrow", [1, E], F32),
        ("ropecs", [H, 128, TPC], F32),
        ("dmask", [4, 128, 512], BF16), ("trimask", [128, 128], BF16),
        ("ident", [128, 128], BF16),
        ("oag_base", [1, 2], I32), ("gla_state_off", [1, 8], I32),
        ("my_prefix_off", [1, 2], I32),
    ]:
        di[nm] = nc.dram_tensor(nm, shp, dt, kind="ExternalInput")

    dscr = dict(
        cc_ln1_in=nc.dram_tensor("cc_ln1_in", [TPC, D], BF16, kind="Internal"),
        cc_ln1_out=nc.dram_tensor("cc_ln1_out", [8 * TPC, D], BF16, kind="Internal", addr_space="Shared"),
        cc_oatt_in=nc.dram_tensor("cc_oatt_in", [2 * S, 130], BF16, kind="Internal"),
        cc_oatt_out=nc.dram_tensor("cc_oatt_out", [8 * 2 * S, 130], BF16, kind="Internal", addr_space="Shared"),
        aa_owno=nc.dram_tensor("aa_owno", [2 * 8 * SLAB, 130], BF16, kind="Internal"),
        cc_gla_in=nc.dram_tensor("cc_gla_in", [2 * H * 128, 65], F32, kind="Internal"),
        cc_gla_out=nc.dram_tensor("cc_gla_out", [8 * 2 * H * 128, 65], F32, kind="Internal", addr_space="Shared"),
        gla_prefix=nc.dram_tensor("gla_prefix", [NSLAB * 128, 1040], F32, kind="Internal"),
        gla_kp=nc.dram_tensor("gla_kp", [H * 128, TPC], BF16, kind="Internal"),
        gla_el=nc.dram_tensor("gla_el", [H * 128, TPC], BF16, kind="Internal"),
        gla_o=nc.dram_tensor("gla_o", [TPC, D], BF16, kind="Internal"),
    )

    with contextlib.ExitStack() as ctx:
        kb = Kb(nc, ctx)
        kb.pe = kb.sem("g_pe")
        kb.dve = kb.sem("g_dve")
        kb.act = kb.sem("g_act")
        kb.cc = kb.sem("g_cc")
        kb.gx = kb.sem("g_gx")
        kb.sx = kb.sem("g_sx")
        kb.g0 = kb.sem("g_g0")
        kb.g1 = kb.sem("g_g1")
        kb.s0 = kb.sem("g_s0")
        kb.s1 = kb.sem("g_s1")
        kb.out = kb.sem("g_out")

        kb.x_res = kb.sb(ctx, "x_res", [128, NT * D], F32)
        kb.ident = kb.sb(ctx, "c_ident", [128, 128], BF16)
        kb.trimask = kb.sb(ctx, "c_trimask", [128, 128], BF16)
        kb.dmask = kb.sb(ctx, "c_dmask", [128, 4 * 512], BF16)
        kb.ones_row = kb.sb(ctx, "c_ones_row", [1, 128], BF16)
        kb.ones_f = kb.sb(ctx, "c_ones_f", [128, SLAB], F32)
        kb.b1cols = kb.sb(ctx, "c_b1cols", [128, 72], F32)
        kb.b3cols = kb.sb(ctx, "c_b3cols", [128, 72], F32)
        kb.nw = kb.sb(ctx, "c_nw", [128, 8], F32)
        kb.offs = kb.sb(ctx, "c_offs", [1, 32], I32)
        kb.stats = kb.sb(ctx, "c_stats", [128, 128], F32)
        kb.ro = ctx.enter_context(nc.gpsimd.register("g_ro"))
        kb.junk = kb.sb(ctx, "c_junk", [128, D], F32)
        kb.junkv = kb.sb(ctx, "c_junkv", [128, 64], F32)

        build_setup(kb, t_x, di)
        if "attn_a" in phases:
            build_attn_a(kb, di, dscr)
        if "ffn1" in phases:
            build_ffn(kb, di, "f1")
        if "attn_b" in phases:
            build_attn_b(kb, di, dscr)
        if "ffn2" in phases:
            build_ffn(kb, di, "f2")
        build_store(kb, t_y)
    return nc


def _dbg_out(kb, name, src_ap, shape, dtype):
    nc = kb.nc
    t = nc.dram_tensor(name, shape, dtype, kind="ExternalOutput")
    with nc.Block("dbg_" + name) as blk:
        @blk.sync
        def _(s):
            kb.out.inc(s.dma_start(t.ap(), src_ap), 16)
            kb.out.wait(s)


def build_setup(kb, t_x, di):
    nc = kb.nc
    with nc.Block("setup") as blk:
        @blk.sync
        def _(s):
            kb.sx.inc(s.dma_start(
                kb.x_res.ap().rearrange("p (t c) -> p t c", c=D),
                t_x.ap().rearrange("(t p) c -> p t c", p=128)), 16)
            kb.sx.inc(s.dma_start(kb.ident[:, :], di["ident"].ap()), 16)
            kb.sx.inc(s.dma_start(kb.trimask[:, :], di["trimask"].ap()), 16)
            kb.sx.inc(s.dma_start(
                kb.dmask.ap().rearrange("p (d q) -> p d q", d=4),
                di["dmask"].ap().rearrange("d p q -> p d q")), 16)
            kb.sx.inc(s.dma_start(kb.b1cols[:, :], di["b1cols"].ap()), 16)
            kb.sx.inc(s.dma_start(kb.b3cols[:, :], di["b3cols"].ap()), 16)
            kb.sx.inc(s.dma_start(kb.nw[:, :], di["nw_cols"].ap()), 16)
            kb.sx.inc(s.dma_start(kb.offs[:, 0:2], di["oag_base"].ap()), 16)
            kb.sx.inc(s.dma_start(kb.offs[:, 16:24], di["gla_state_off"].ap()), 16)
            kb.sx.inc(s.dma_start(kb.offs[:, 24:26], di["my_prefix_off"].ap()), 16)
            kb.sx.wait(s)
        @blk.vector
        def _(v):
            v.memset(kb.ones_row[:, :], 1.0)
            v.memset(kb.ones_f[:, :], 1.0)


def emit_ln(kb, tag, ln_tm, lnT=None, pst=None, ship=None, strace=None):
    nc = kb.nc
    stats, junk, x_res = kb.stats, kb.junk, kb.x_res
    pe0, dve0, act0 = kb.pe.v, kb.dve.v, kb.act.v
    s0v, s1v = kb.s0.v, kb.s1.v
    with nc.Block(f"ln_{tag}") as blk:
        @blk.vector
        def _(v):
            for t in range(NT):
                ins = v.reduce_sum(stats[:, t:t + 1], x_res[:, t * D:(t + 1) * D],
                                   axis=AX)
            kb.dve.inc(ins)
            kb.dve.wait(v)
            kb.dve.inc(v.tensor_scalar(stats[:, 32:48], stats[:, 0:16],
                                       1.0 / D, None, ALU.mult))
            kb.dve.wait(v)
            kb.dve.inc(v.scalar_tensor_tensor(stats[:, 48:64], stats[:, 32:48],
                                              -1.0, stats[:, 32:48],
                                              ALU.mult, ALU.mult))
            kb.dve.wait(v)
            kb.act.wait(v, act0 + NT + 2)
            kb.dve.inc(v.scalar_tensor_tensor(stats[:, 64:80], stats[:, 16:32],
                                              1.0 / D, stats[:, 48:64],
                                              ALU.mult, ALU.add))
            kb.dve.wait(v)
            kb.dve.inc(v.tensor_copy(kb.junkv[:, 0:16], stats[:, 64:80]))
            kb.act.wait(v, act0 + NT + 5)
            kb.dve.inc(v.scalar_tensor_tensor(stats[:, 112:128], stats[:, 32:48],
                                              -1.0, stats[:, 96:112],
                                              ALU.mult, ALU.mult))
            kb.dve.wait(v)
            for t in range(NT):
                if ship is not None and t >= 2:
                    sl = kb.s0 if t % 2 == 0 else kb.s1
                    slv = s0v if t % 2 == 0 else s1v
                    sl.wait(v, slv + 16 * (t // 2))
                kb.dve.inc(v.tensor_scalar(
                    ln_tm[:, (t % 2) * D:(t % 2) * D + D],
                    x_res[:, t * D:(t + 1) * D],
                    stats[:, 96 + t:97 + t], stats[:, 112 + t:113 + t],
                    ALU.mult, ALU.add))
                if lnT is not None and t > 0:
                    kb.pe.wait(v, pe0 + t)
                    kb.dve.inc(v.tensor_copy(
                        lnT.ap().rearrange("p (k n) -> p k n", k=8)[:, :, (t - 1) * 128:t * 128],
                        pst[(t - 1) % 2][:, :].rearrange("p (k n) -> p k n", k=8)))
            if lnT is not None:
                kb.pe.wait(v, pe0 + NT)
                kb.dve.inc(v.tensor_copy(
                    lnT.ap().rearrange("p (k n) -> p k n", k=8)[:, :, (NT - 1) * 128:NT * 128],
                    pst[(NT - 1) % 2][:, :].rearrange("p (k n) -> p k n", k=8)))

        @blk.scalar
        def _(sc):
            for t in range(NT):
                if t > 0:
                    kb.act.wait(sc)
                kb.act.inc(sc.activation(junk[:, :], x_res[:, t * D:(t + 1) * D],
                                         AF.Square, accum_out=stats[:, 16 + t:17 + t]))
            kb.act.wait(sc)
            kb.act.inc(sc.activation(junk[:, 0:16], x_res[:, 0:16], AF.Square))
            kb.act.wait(sc)
            kb.act.inc(sc.activation(junk[:, 0:16], x_res[:, 0:16], AF.Square))
            kb.dve.wait(sc, dve0 + 5)
            kb.act.inc(sc.activation(stats[:, 80:96], stats[:, 64:80], AF.Ln,
                                     bias=f32(EPS_LN)))
            kb.act.wait(sc)
            kb.act.inc(sc.activation(stats[:, 96:112], stats[:, 80:96], AF.Exp,
                                     scale=f32(-0.5)))
            kb.act.wait(sc)
            kb.act.inc(sc.activation(junk[:, 0:16], x_res[:, 0:16], AF.Square))

        if lnT is not None:
            @blk.tensor
            def _(t_):
                napply = 0
                for t in range(NT):
                    kb.dve.wait(t_, dve0 + 6 + (2 * t + 1 if t > 0 else 1))
                    if t >= 2:
                        kb.dve.wait(t_, dve0 + 6 + 2 * (t - 1) + 2)
                    for k in range(8):
                        ins = t_.transpose(
                            pst[t % 2][:, k * 128:(k + 1) * 128],
                            ln_tm[:, (t % 2) * D + k * 128:(t % 2) * D + (k + 1) * 128],
                            kb.ident[:, :])
                        if k == 7:
                            kb.pe.inc(ins)
        if ship is not None:
            @blk.sync
            def _(s):
                for t in range(NT):
                    kb.dve.wait(s, dve0 + 6 + t + 1)
                    sl = kb.s0 if t % 2 == 0 else kb.s1
                    sl.inc(s.dma_start(
                        ship.ap()[t * 128:(t + 1) * 128, :],
                        ln_tm[:, (t % 2) * D:(t % 2) * D + D]), 16)
                kb.s0.wait(s)
                kb.s1.wait(s)



def build_attn_a(kb, di, dscr):
    nc = kb.nc
    cc_ln1_in, cc_ln1_out = dscr["cc_ln1_in"], dscr["cc_ln1_out"]
    cc_oatt_in, cc_oatt_out = dscr["cc_oatt_in"], dscr["cc_oatt_out"]

    with contextlib.ExitStack() as pctx:
        ln_tm = kb.sb(pctx, "aa_lntm", [128, 2 * D], BF16)
        strace = kb.sb(pctx, "aa_strace", [128, NT * 10], F32) if "strace" in DBG else None
        emit_ln(kb, "aa", ln_tm, ship=cc_ln1_in, strace=strace)
        if strace is not None:
            _dbg_out(kb, "dbg_strace", strace.ap(), [128, NT * 10], F32)
        with nc.Block("aa_ag1") as blk:
            @blk.gpsimd
            def _(g):
                kb.cc.inc(g.collective_compute(
                    "AllGather", ALU.bypass, replica_groups=[list(range(8))],
                    ins=[cc_ln1_in.ap().opt()], outs=[cc_ln1_out.ap().opt()]))
                kb.cc.wait(g)
        if "ln1" in DBG:
            _dbg_out(kb, "dbg_ln1", cc_ln1_in.ap(), [TPC, D], BF16)
        if AA_STAGES < 2:
            return

    with contextlib.ExitStack() as pctx:
        Q2 = kb.sb(pctx, "aa_Q2", [128, S], BF16)
        K2 = kb.sb(pctx, "aa_K2", [128, S], BF16)
        V2 = kb.sb(pctx, "aa_V2", [128, 64 * 130], BF16)
        O2 = kb.sb(pctx, "aa_O2", [128, 64 * 130], F32)
        lnTb = kb.sb(pctx, "aa_lnTb", [128, 2 * 8192], BF16)
        wqs = kb.sb(pctx, "aa_wqs", [128, 8 * 128], BF16)
        wks = kb.sb(pctx, "aa_wks", [128, 8 * 128], BF16)
        wvs = kb.sb(pctx, "aa_wvs", [128, 8 * 128], BF16)
        P_sb = kb.sb(pctx, "aa_Psb", [128, 4 * 1024], BF16)

        with nc.Block("aa_wload") as blk:
            @blk.gpsimd
            def _(g):
                for nm, dst in (("wq_a_sl", wqs), ("wk_a_sl", wks), ("wv_a_sl", wvs)):
                    kb.gx.inc(g.dma_start(
                        dst.ap().rearrange("p (k n) -> p k n", k=8),
                        di[nm].ap().rearrange("(k p) n -> p k n", p=128)), 16)
                kb.gx.wait(g)
            @blk.vector
            def _(v):
                v130 = V2.ap().rearrange("p (t c) -> p t c", c=130)
                kb.dve.inc(v.memset(V2[:, :], 0.0))
                kb.dve.wait(v)
                v.memset(v130[:, :, 64:65], 1.0)
                v.memset(v130[:, :, 129:130], 1.0)
                v.memset(O2[:, :], 0.0)

        for unit in range(2 if AA_STAGES >= 3 else 1):
            with contextlib.ExitStack() as qctx:
                pq = kb.ps(qctx, f"aa_pq{unit}", [128, 1024], F32)
                pk = kb.ps(qctx, f"aa_pk{unit}", [128, 1024], F32)
                pv = [kb.ps(qctx, f"aa_pv{unit}{i}", [128, 512], F32) for i in range(2)]
                emit_attn_proj(kb, unit, cc_ln1_out, lnTb, wqs, wks, wvs,
                               Q2, K2, V2, pq, pk, pv)
            if AA_STAGES < 3:
                if "qkv" in DBG:
                    _dbg_out(kb, "dbg_q2", Q2.ap(), [128, S], BF16)
                    _dbg_out(kb, "dbg_k2", K2.ap(), [128, S], BF16)
                    _dbg_out(kb, "dbg_v2", V2.ap(), [128, 64 * 130], BF16)
                return
            with contextlib.ExitStack() as qctx:
                pS = [kb.ps(qctx, f"aa_pS{unit}{i}", [128, 1024], F32) for i in range(2)]
                pO = [kb.ps(qctx, f"aa_pO{unit}{i}", [128, 512], F32) for i in range(2)]
                emit_attn_unit(kb, unit, Q2, K2, V2, O2, P_sb, pS, pO, cc_oatt_in)

        if "oatt" in DBG:
            _dbg_out(kb, "dbg_oatt", cc_oatt_in.ap(), [2 * S, 130], BF16)
        if AA_STAGES < 4:
            return
        with nc.Block("aa_oag") as blk:
            @blk.gpsimd
            def _(g):
                kb.gx.wait(g)
                kb.cc.inc(g.collective_compute(
                    "AllGather", ALU.bypass, replica_groups=[list(range(8))],
                    ins=[cc_oatt_in.ap().opt()], outs=[cc_oatt_out.ap().opt()]))
                kb.cc.wait(g)

    with contextlib.ExitStack() as pctx:
        emit_attn_outproj(kb, pctx, di, cc_oatt_out, dscr["aa_owno"])


def emit_attn_proj(kb, unit, cc_ln1_out, lnTb, wqs, wks, wvs, Q2, K2, V2, pq, pk, pv):
    nc = kb.nc

    def issue_loads(s, b):
        j2, half = _owner(b)
        row0 = (4 * unit + j2) * TPC + half * SLAB
        slot = kb.s0 if b % 2 == 0 else kb.s1
        for kchunk in range(8):
            slot.inc(s.dma_start(
                lnTb[:, (b % 2) * 8192 + kchunk * 1024:(b % 2) * 8192 + (kchunk + 1) * 1024],
                cc_ln1_out.ap()[row0:row0 + SLAB, kchunk * 128:(kchunk + 1) * 128],
                transpose=True), 16)
        return slot.v

    tgt = {}
    with nc.Block(f"aa_u{unit}_pl") as blk:
        @blk.sync
        def _(s):
            tgt[0] = issue_loads(s, 0)
    for b in range(AA_NPB):
        pe0, dve0 = kb.pe.v, kb.dve.v
        with nc.Block(f"aa_u{unit}_p{b}") as blk:
            @blk.sync
            def _(s, b=b):
                if b + 1 < AA_NPB:
                    tgt[b + 1] = issue_loads(s, b + 1)
            @blk.tensor
            def _(t_, b=b):
                slot = kb.s0 if b % 2 == 0 else kb.s1
                slot.wait(t_, tgt[b])
                lb = lnTb[:, (b % 2) * 8192:(b % 2) * 8192 + 8192]
                ins = None
                for k in range(8):
                    for half2 in range(2):
                        t_.matmul(pq[:, half2 * 512:half2 * 512 + 512],
                                  wqs[:, k * 128:(k + 1) * 128],
                                  lb[:, k * 1024 + half2 * 512:k * 1024 + half2 * 512 + 512],
                                  start=(k == 0), stop=(k == 7))
                for k in range(8):
                    for half2 in range(2):
                        ins = t_.matmul(pk[:, half2 * 512:half2 * 512 + 512],
                                        wks[:, k * 128:(k + 1) * 128],
                                        lb[:, k * 1024 + half2 * 512:k * 1024 + half2 * 512 + 512],
                                        start=(k == 0), stop=(k == 7))
                kb.pe.inc(ins)
                for tt in range(8):
                    for k in range(8):
                        ins = t_.matmul(pv[tt // 4][:, (tt % 4) * 128:(tt % 4) * 128 + 128],
                                        lb[:, k * 1024 + tt * 128:k * 1024 + (tt + 1) * 128],
                                        wvs[:, k * 128:(k + 1) * 128],
                                        start=(k == 0), stop=(k == 7))
                    if tt == 3 or tt == 7:
                        kb.pe.inc(ins)
            @blk.vector
            def _(v, b=b, pe0=pe0):
                kb.pe.wait(v, pe0 + 1)
                v.tensor_copy(Q2[:, b * 1024:(b + 1) * 1024], pq[:, :])
                kb.dve.inc(v.tensor_copy(K2[:, b * 1024:(b + 1) * 1024], pk[:, :]))
                for g2 in range(2):
                    kb.pe.wait(v, pe0 + 2 + g2)
                    for tt4 in range(4):
                        gt = b * 8 + g2 * 4 + tt4
                        v.tensor_copy(V2[:, gt * 130:gt * 130 + 64],
                                      pv[g2][:, tt4 * 128:tt4 * 128 + 64])
                        ins = v.tensor_copy(V2[:, gt * 130 + 65:gt * 130 + 129],
                                            pv[g2][:, tt4 * 128 + 64:tt4 * 128 + 128])
                    kb.dve.inc(ins)


def emit_attn_unit(kb, unit, Q2, K2, V2, O2, P_sb, pS, pO, cc_oatt_in):
    nc = kb.nc
    NQG = S // 512
    for qg in range(NQG):
        nk = 4 * qg + 4
        pe0, act0, dve0 = kb.pe.v, kb.act.v, kb.dve.v
        with nc.Block(f"aa_u{unit}_q{qg}") as blk:
            @blk.tensor
            def _(t_, qg=qg, nk=nk, pe0=pe0, act0=act0, dve0=dve0):
                for s in range(nk + 2):
                    ins = None
                    if s < nk:
                        kt = s
                        ps_ = pS[kt % 2]
                        if kt >= 2:
                            kb.act.wait(t_, act0 + kt - 1)
                        for h in range(2):
                            ins = t_.matmul(ps_[:, h * 512:h * 512 + 512],
                                            K2[h * 64:h * 64 + 64, kt * 128:(kt + 1) * 128],
                                            Q2[h * 64:h * 64 + 64, qg * 512:(qg + 1) * 512],
                                            start=True, stop=True)
                    if s >= 2:
                        kt = s - 2
                        d = kt - 4 * qg
                        if d >= 0:
                            kb.dve.wait(t_, dve0 + d + 1)
                        else:
                            kb.act.wait(t_, act0 + kt + 1)
                        pb = P_sb[:, (kt % 4) * 1024:(kt % 4) * 1024 + 1024]
                        for h in range(2):
                            for qt in range(4):
                                ins = t_.matmul(
                                    pO[qt // 2][:, (qt % 2) * 256 + h * 65:(qt % 2) * 256 + h * 65 + 65],
                                    pb[:, h * 512 + qt * 128:h * 512 + (qt + 1) * 128],
                                    V2[:, kt * 130 + h * 65:kt * 130 + h * 65 + 65],
                                    start=(kt == 0 and h == 0 and qt % 2 == 0),
                                    stop=(kt == nk - 1 and h == 1 and qt % 2 == 1))
                    kb.pe.inc(ins)

            @blk.scalar
            def _(sc, nk=nk, pe0=pe0):
                for kt in range(nk):
                    ps_ = pS[kt % 2]
                    kb.pe.wait(sc, pe0 + kt + 1)
                    kb.act.inc(sc.activation(
                        P_sb[:, (kt % 4) * 1024:(kt % 4) * 1024 + 1024],
                        ps_[:, :], AF.Exp, bias=f32(-MSHIFT), scale=f32(0.125)))

            @blk.vector
            def _(v, qg=qg, nk=nk, act0=act0, pe0=pe0):
                for d in range(4):
                    kt = 4 * qg + d
                    kb.act.wait(v, act0 + kt + 1)
                    pb = P_sb[:, (kt % 4) * 1024:(kt % 4) * 1024 + 1024]
                    for h in range(2):
                        ins = v.tensor_tensor(
                            pb[:, h * 512:h * 512 + 512],
                            pb[:, h * 512:h * 512 + 512],
                            kb.dmask[:, d * 512:(d + 1) * 512], ALU.mult)
                        if h == 1:
                            kb.dve.inc(ins)
                kb.pe.wait(v, pe0 + nk + 2)
                for qt in range(4):
                    gt = qg * 4 + qt
                    kb.dve.inc(v.tensor_tensor(
                        O2[:, gt * 130:(gt + 1) * 130], O2[:, gt * 130:(gt + 1) * 130],
                        pO[qt // 2][:, (qt % 2) * 256:(qt % 2) * 256 + 130], ALU.add))

    with nc.Block(f"aa_ship_o{unit}") as blk:
        @blk.gpsimd
        def _(g):
            kb.gx.inc(g.dma_start(
                cc_oatt_in.ap()[unit * S:(unit + 1) * S, :].rearrange(
                    "(t p) c -> p t c", p=128),
                O2.ap().rearrange("p (t c) -> p t c", c=130)), 16)
            kb.gx.wait(g)
    if unit == 0:
        with nc.Block("aa_zero") as blk:
            @blk.vector
            def _(v):
                v.memset(O2[:, :], 0.0)


def emit_attn_outproj(kb, pctx, di, cc_oatt_out, owno):
    nc = kb.nc
    woa = kb.sb(pctx, "aa_woa", [128, 8 * 1024], BF16)
    oasm = kb.sb(pctx, "aa_oasm", [128, 2 * 8 * 130], F32)
    osc = kb.sb(pctx, "aa_osc", [128, 1024], BF16)
    ofm = kb.sb(pctx, "aa_ofm", [128, 1024], BF16)
    rcp = kb.sb(pctx, "aa_rcp", [128, 16], F32)
    pt = kb.ps(pctx, "aa_pt", [128, 1024], BF16)
    po = kb.ps(pctx, "aa_po", [128, 1024], F32)

    with nc.Block("aa_wo_load") as blk:
        @blk.gpsimd
        def _(g):
            kb.gx.inc(g.dma_start(
                woa.ap().rearrange("p (k n) -> p k n", k=8),
                di["wo_a"].ap().rearrange("(k p) n -> p k n", p=128)), 16)
            for sl in range(2):
                g.reg_load(kb.ro, kb.offs[0:1, sl:sl + 1])
                kb.gx.inc(g.dma_start(
                    owno.ap()[sl * 8 * SLAB:(sl + 1) * 8 * SLAB, :].rearrange(
                        "(s r) c -> s r c", s=8),
                    bass.AP(cc_oatt_out, kb.ro,
                            [[2 * S * 130, 8], [130, SLAB], [1, 130]])), 16)
            kb.gx.wait(g)

    pe0, dve0, gx0 = kb.pe.v, kb.dve.v, kb.gx.v
    NDV = 4
    with nc.Block("aa_op") as blk:
        @blk.gpsimd
        def _(g):
            for t in range(NT):
                sl, tt = t // 8, t % 8
                base = sl * 8 * SLAB + tt * 128
                if t > 0:
                    kb.gx.wait(g, gx0 + t * 16)
                if t >= 2:
                    kb.dve.wait(g, dve0 + (t - 2) * NDV + 2)
                kb.gx.inc(g.dma_start(
                    oasm[:, (t % 2) * 1040:(t % 2) * 1040 + 1040].rearrange("p (s c) -> p s c", c=130),
                    bass.AP(owno, base * 130,
                            [[130, 128], [SLAB * 130, 8], [1, 130]])), 16)
        @blk.vector
        def _(v):
            for t in range(NT):
                kb.gx.wait(v, gx0 + (t + 1) * 16)
                if t > 0:
                    kb.dve.wait(v)
                ob = (t % 2) * 8 * 130
                for h in range(16):
                    col = ob + (h // 2) * 130 + (h % 2) * 65
                    ins = v.reciprocal(rcp[:, h:h + 1], oasm[:, col + 64:col + 65])
                kb.dve.inc(ins)
                kb.dve.wait(v)
                for h in range(16):
                    col = ob + (h // 2) * 130 + (h % 2) * 65
                    ins = v.tensor_scalar(osc[:, h * 64:(h + 1) * 64],
                                          oasm[:, col:col + 64],
                                          rcp[:, h:h + 1], None, ALU.mult)
                kb.dve.inc(ins)
                kb.pe.wait(v, pe0 + t * 2 + 1)
                kb.dve.inc(v.tensor_copy(ofm[:, :], pt[:, :]))
                kb.pe.wait(v, pe0 + t * 2 + 2)
                kb.dve.inc(v.tensor_tensor(
                    kb.x_res[:, t * D:(t + 1) * D],
                    kb.x_res[:, t * D:(t + 1) * D], po[:, :], ALU.add))
        @blk.tensor
        def _(t_):
            for t in range(NT):
                kb.dve.wait(t_, dve0 + t * NDV + 2)
                if t > 0:
                    kb.dve.wait(t_, dve0 + (t - 1) * NDV + 3)
                for hp in range(8):
                    ins = t_.transpose(pt[:, hp * 128:(hp + 1) * 128],
                                       osc[:, hp * 128:(hp + 1) * 128],
                                       kb.ident[:, :])
                kb.pe.inc(ins)
                kb.dve.wait(t_, dve0 + t * NDV + 3)
                if t > 0:
                    kb.dve.wait(t_, dve0 + (t - 1) * NDV + 4)
                for k in range(8):
                    for half in range(2):
                        ins = t_.matmul(
                            po[:, half * 512:half * 512 + 512],
                            ofm[:, k * 128:(k + 1) * 128],
                            woa[:, k * 1024 + half * 512:k * 1024 + half * 512 + 512],
                            start=(k == 0), stop=(k == 7))
                kb.pe.inc(ins)



def build_ffn(kb, di, tag):
    nc = kb.nc
    with contextlib.ExitStack() as pctx:
        lnT = kb.sb(pctx, f"{tag}_lnT", [128, 8 * TPC], BF16)
        ln_tm = kb.sb(pctx, f"{tag}_lntm", [128, 2 * D], BF16)
        gatew = kb.sb(pctx, f"{tag}_gatew", [128, NT * 16], F32)
        shctx = contextlib.ExitStack()
        wsh = kb.sb(shctx, f"{tag}_wsh", [128, 3 * 8192], BF16)
        b2sh = kb.sb(shctx, f"{tag}_b2sh", [1, D], BF16)
        shw_tgt = [0]
        with nc.Block(f"{tag}_shw") as blk:
            @blk.gpsimd
            def _(g):
                kb.gx.inc(g.dma_start(
                    wsh.ap().rearrange("p (xx n) -> p xx n", n=1024)[:, 0:8, :],
                    di["w1t"].ap()[0:D, :].rearrange("(k p) n -> p k n", p=128)), 16)
                kb.gx.inc(g.dma_start(
                    wsh.ap().rearrange("p (xx n) -> p xx n", n=1024)[:, 8:16, :],
                    di["w3t"].ap()[0:D, :].rearrange("(k p) n -> p k n", p=128)), 16)
                kb.gx.inc(g.dma_start(
                    wsh.ap().rearrange("p (xx n) -> p xx n", n=1024)[:, 16:24, :],
                    di["w2t"].ap()[0:FDFF, :].rearrange("(k p) n -> p k n", p=128)), 16)
                kb.gx.inc(g.dma_start(b2sh[:, :], di["b2rows"].ap()[0:1, :]), 16)
                shw_tgt[0] = kb.gx.v
        with contextlib.ExitStack() as qctx:
            pst = [kb.ps(qctx, f"{tag}_pst{i}", [128, 1024], BF16) for i in range(2)]
            emit_ln(kb, tag, ln_tm, lnT=lnT, pst=pst)
        with contextlib.ExitStack() as qctx:
            emit_gating(kb, qctx, di, tag, lnT, gatew)
        with contextlib.ExitStack() as qctx:
            hs = kb.sb(qctx, f"{tag}_hs_sh", [128, 1024], BF16)
            h_sb = kb.sb(qctx, f"{tag}_h_sh", [128, 2 * 8 * 512], BF16)
            ph = [kb.ps(qctx, f"{tag}_phs{i}", [128, 512], F32) for i in range(4)]
            py = [kb.ps(qctx, f"{tag}_pys{i}", [128, 1024], F32) for i in range(2)]
            emit_expert_core(kb, f"{tag}_sh", lnT, gatew, None, wsh, b2sh,
                             hs, h_sb, ph, py, dff=FDFF, bcol0=0,
                             wslot=(kb.gx, kb.gx.v))
        shctx.close()

        with contextlib.ExitStack() as qctx:
            wbuf = [kb.sb(qctx, f"{tag}_w{i}", [128, 3 * 4096], BF16) for i in range(2)]
            b2r = [kb.sb(qctx, f"{tag}_b2r{i}", [1, D], BF16) for i in range(2)]
            hs = kb.sb(qctx, f"{tag}_hs", [128, 1024], BF16)
            h_sb = kb.sb(qctx, f"{tag}_h", [128, 2 * 4 * 512], BF16)
            ph = [kb.ps(qctx, f"{tag}_ph{i}", [128, 512], F32) for i in range(4)]
            py = [kb.ps(qctx, f"{tag}_py{i}", [128, 1024], F32) for i in range(2)]
            gslot = [kb.g0, kb.g1]

            def wload(g, e):
                i = e % 2
                sl = gslot[i]
                r0 = (1 + e) * D
                r2 = FDFF + e * DFF
                sl.inc(g.dma_start(
                    wbuf[i].ap().rearrange("p (xx n) -> p xx n", n=512)[:, 0:8, :],
                    di["w1t"].ap()[r0:r0 + D, 0:DFF].rearrange("(k p) n -> p k n", p=128)), 16)
                sl.inc(g.dma_start(
                    wbuf[i].ap().rearrange("p (xx n) -> p xx n", n=512)[:, 8:16, :],
                    di["w3t"].ap()[r0:r0 + D, 0:DFF].rearrange("(k p) n -> p k n", p=128)), 16)
                sl.inc(g.dma_start(
                    wbuf[i].ap().rearrange("p (xx n) -> p xx n", n=1024)[:, 8:12, :],
                    di["w2t"].ap()[r2:r2 + DFF, :].rearrange("(k p) n -> p k n", p=128)), 16)
                sl.inc(g.dma_start(b2r[i][:, :], di["b2rows"].ap()[1 + e:2 + e, :]), 16)

            with nc.Block(f"{tag}_pre") as blk:
                @blk.gpsimd
                def _(g):
                    wload(g, 0)

            emit_experts_merged(kb, tag, lnT, gatew, wbuf, b2r, hs, h_sb, ph, py,
                                gslot, wload)


def emit_experts_merged(kb, tag, lnT, gatew, wbuf, b2r, hs, h_sb, ph, py,
                        gslot, wload):
    nc = kb.nc
    nm = DFF // 128
    w1o, w3o, w2o = 0, 8 * DFF, 16 * DFF
    pe0, dve0, act0 = kb.pe.v, kb.dve.v, kb.act.v
    x_res = kb.x_res
    sched = []
    for ch in range(5):
        if ch < 4:
            for m in range(nm):
                if ch > 0 and m == 2:
                    for tt in range(4):
                        sched.append(("y", ch - 1, tt))
                sched.append(("h", ch, m))
        else:
            for tt in range(4):
                sched.append(("y", 3, tt))
    pe_idx = {ev: i + 1 for i, ev in enumerate(sched)}
    PE_PER = len(sched)
    dve_idx = {}
    di_ = 0
    for ch in range(4):
        for m in range(nm):
            dve_idx[("c", ch, m)] = (di_ := di_ + 1)
        for tt in range(4):
            dve_idx[("e", ch, tt)] = (di_ := di_ + 1)
    DVE_PER = di_
    act_idx = {}
    ai_ = 0
    for ch in range(4):
        for m in range(nm):
            act_idx[(ch, m)] = (ai_ := ai_ + 1)
    ACT_PER = ai_
    wtgts = [gslot[e % 2].v + (0 if e < 1 else 64 * ((e - 1) // 2 + 1))
             for e in range(E)]
    w0 = [gslot[0].v, gslot[1].v]
    def wtgt(e):
        i = e % 2
        nloads = e // 2 + 1
        return w0[i] + (nloads - (1 if i == 0 else 0)) * 64 + (0 if i == 1 else 0) \
            if False else w0[i] + (e // 2 + (1 if i == 0 else 1)) * 64 - 64

    planned = [None] * E
    planned[0] = w0[0]
    with nc.Block(f"{tag}_emerged") as blk:
        @blk.gpsimd
        def _(g):
            for e in range(1, E):
                if e >= 2:
                    g.wait_ge(kb.pe.h, pe0 + (e - 2) * PE_PER + PE_PER)
                wload(g, e)
                planned[e] = gslot[e % 2].v

        @blk.tensor
        def _(t_):
            for e in range(E):
                ebase_pe = pe0 + e * PE_PER
                ebase_dve = dve0 + e * DVE_PER
                ebase_act = act0 + e * ACT_PER
                gslot[e % 2].wait(t_, planned[e] if planned[e] is not None else w0[0])
                for ev in sched:
                    if ev[0] == "h":
                        _, ch, m = ev
                        gm = ch * nm + m
                        if gm >= 2:
                            pch, pm = (ch, m - 2) if m >= 2 else (ch - 1, nm - 2 + m)
                            kb.act.wait(t_, ebase_act + act_idx[(pch, pm)])
                            kb.dve.wait(t_, ebase_dve + dve_idx[("c", pch, pm)])
                        elif e > 0:
                            pch, pm = 3, nm - 2 + m
                            kb.act.wait(t_, ebase_act - ACT_PER + act_idx[(pch, pm)])
                            kb.dve.wait(t_, ebase_dve - DVE_PER + dve_idx[("c", pch, pm)])
                        tok0 = ch * 512
                        wb = wbuf[e % 2]
                        for k in range(8):
                            t_.matmul(ph[m % 2][:, :],
                                      wb[:, w1o + k * DFF + m * 128:w1o + k * DFF + (m + 1) * 128],
                                      lnT[:, k * TPC + tok0:k * TPC + tok0 + 512],
                                      start=(k == 0), stop=(k == 7))
                        for k in range(8):
                            ins = t_.matmul(ph[2 + m % 2][:, :],
                                            wb[:, w3o + k * DFF + m * 128:w3o + k * DFF + (m + 1) * 128],
                                            lnT[:, k * TPC + tok0:k * TPC + tok0 + 512],
                                            start=(k == 0), stop=(k == 7))
                        kb.pe.inc(ins)
                    else:
                        _, ch, tt = ev
                        gt = ch * 4 + tt
                        kb.dve.wait(t_, ebase_dve + dve_idx[("c", ch, nm - 1)])
                        if gt >= 2:
                            kb.dve.wait(t_, ebase_dve + dve_idx[("e", (gt - 2) // 4, (gt - 2) % 4)])
                        elif e > 0:
                            kb.dve.wait(t_, ebase_dve - DVE_PER + dve_idx[("e", 3, 2 + gt)])
                        pyb = py[tt % 2]
                        hoff = (ch % 2) * nm * 512
                        wb = wbuf[e % 2]
                        for k in range(nm):
                            for hf in range(2):
                                t_.matmul(pyb[:, hf * 512:hf * 512 + 512],
                                          h_sb[:, hoff + k * 512 + tt * 128:hoff + k * 512 + tt * 128 + 128],
                                          wb[:, w2o + k * 1024 + hf * 512:w2o + k * 1024 + hf * 512 + 512],
                                          start=(k == 0), stop=False)
                        for hf in range(2):
                            ins = t_.matmul(pyb[:, hf * 512:hf * 512 + 512],
                                            kb.ones_row[:, :],
                                            b2r[e % 2][:, hf * 512:hf * 512 + 512],
                                            start=False, stop=True)
                        kb.pe.inc(ins)

        @blk.scalar
        def _(sc):
            for e in range(E):
                ebase_pe = pe0 + e * PE_PER
                ebase_dve = dve0 + e * DVE_PER
                for ch in range(4):
                    for m in range(nm):
                        kb.pe.wait(sc, ebase_pe + pe_idx[("h", ch, m)])
                        gm = ch * nm + m
                        if gm >= 2:
                            pch, pm = (ch, m - 2) if m >= 2 else (ch - 1, nm - 2 + m)
                            kb.act.wait(sc)
                            kb.dve.wait(sc, ebase_dve + dve_idx[("c", pch, pm)])
                        elif e > 0:
                            pch, pm = 3, nm - 2 + m
                            kb.act.wait(sc)
                            kb.dve.wait(sc, ebase_dve - DVE_PER + dve_idx[("c", pch, pm)])
                        kb.act.inc(sc.activation(hs[:, (m % 2) * 512:(m % 2) * 512 + 512],
                                                 ph[m % 2][:, :], AF.Silu,
                                                 bias=kb.b1cols[:, 8 + e * 4 + m:8 + e * 4 + m + 1]))

        @blk.vector
        def _(v):
            for e in range(E):
                ebase_pe = pe0 + e * PE_PER
                ebase_act = act0 + e * ACT_PER
                for ch in range(4):
                    if ch >= 2 or e > 0:
                        kb.dve.wait(v)
                        if ch >= 2:
                            kb.pe.wait(v, ebase_pe + pe_idx[("y", ch - 2, 3)])
                        else:
                            kb.pe.wait(v, ebase_pe - PE_PER + pe_idx[("y", ch + 2, 3)])
                    for m in range(nm):
                        kb.act.wait(v, ebase_act + act_idx[(ch, m)])
                        kb.dve.inc(v.scalar_tensor_tensor(
                            h_sb[:, (ch % 2) * nm * 512 + m * 512:(ch % 2) * nm * 512 + (m + 1) * 512],
                            ph[2 + m % 2][:, :],
                            kb.b3cols[:, 8 + e * 4 + m:8 + e * 4 + m + 1],
                            hs[:, (m % 2) * 512:(m % 2) * 512 + 512],
                            ALU.add, ALU.mult))
                    for tt in range(4):
                        gt = ch * 4 + tt
                        kb.pe.wait(v, ebase_pe + pe_idx[("y", ch, tt)])
                        pyb = py[tt % 2]
                        kb.dve.inc(v.scalar_tensor_tensor(
                            x_res[:, gt * D:(gt + 1) * D],
                            pyb[:, :],
                            gatew[:, gt * 16 + e:gt * 16 + e + 1],
                            x_res[:, gt * D:(gt + 1) * D],
                            ALU.mult, ALU.add))


def emit_gating(kb, qctx, di, tag, lnT, gatew):
    nc = kb.nc
    gwt = kb.sb(qctx, f"{tag}_gwt", [128, 8 * 16], BF16)
    ebr = kb.sb(qctx, f"{tag}_ebr", [1, 16], BF16)
    ebb = kb.sb(qctx, f"{tag}_ebb", [128, 16], F32)
    logit = kb.sb(qctx, f"{tag}_logit", [128, 16], F32)
    gsc = kb.sb(qctx, f"{tag}_gsc", [128, 80], F32)
    pg = kb.ps(qctx, f"{tag}_pg", [128, 16], F32)

    with nc.Block(f"{tag}_gw") as blk:
        @blk.gpsimd
        def _(g):
            kb.gx.inc(g.dma_start(
                gwt.ap().rearrange("p (k n) -> p k n", k=8),
                di["gate_wt"].ap().rearrange("(k p) n -> p k n", p=128)), 16)
            kb.gx.inc(g.dma_start(ebr[:, :], di["ebrow"].ap()), 16)
            kb.gx.wait(g)
        @blk.tensor
        def _(t_):
            kb.gx.wait(t_)
            kb.pe.inc(t_.matmul(pg[:, :], kb.ones_row[:, :], ebr[:, :],
                                start=True, stop=True))
        @blk.vector
        def _(v):
            kb.pe.wait(v)
            kb.dve.inc(v.tensor_copy(ebb[:, :], pg[:, :]))

    pe0, dve0, act0 = kb.pe.v, kb.dve.v, kb.act.v
    NDV = 17
    with nc.Block(f"{tag}_gate") as blk:
        @blk.tensor
        def _(t_):
            for t in range(NT):
                if t > 0:
                    kb.dve.wait(t_, dve0 + (t - 1) * NDV + 1)
                for k in range(8):
                    t_.matmul(pg[:, :], lnT[:, k * TPC + t * 128:k * TPC + (t + 1) * 128],
                              gwt[:, k * 16:(k + 1) * 16],
                              start=(k == 0), stop=False)
                kb.pe.inc(t_.matmul(pg[:, :], kb.ones_row[:, :], ebr[:, :],
                                    start=False, stop=True))
        @blk.vector
        def _(v):
            for t in range(NT):
                kb.pe.wait(v, pe0 + t + 1)
                if t > 0:
                    kb.dve.wait(v)
                    kb.act.wait(v, act0 + t)
                kb.dve.inc(v.tensor_copy(gsc[:, 0:16], pg[:, :]))
                kb.dve.wait(v)
                kb.dve.inc(v.tensor_tensor(logit[:, :], gsc[:, 0:16], ebb[:, :],
                                           ALU.subtract))
                kb.dve.inc(v.reduce_max(gsc[:, 64:65], gsc[:, 0:16], axis=AX))
                kb.dve.wait(v)
                kb.dve.inc(v.tensor_scalar(gsc[:, 16:32], gsc[:, 0:16],
                                           gsc[:, 64:65], 0.0, ALU.is_equal, ALU.add))
                kb.dve.wait(v)
                kb.dve.inc(v.scalar_tensor_tensor(gsc[:, 48:64], gsc[:, 16:32], -1e30,
                                                  gsc[:, 0:16], ALU.mult, ALU.add))
                kb.dve.wait(v)
                kb.dve.inc(v.reduce_max(gsc[:, 65:66], gsc[:, 48:64], axis=AX))
                kb.dve.wait(v)
                kb.dve.inc(v.tensor_scalar(gsc[:, 32:48], gsc[:, 48:64],
                                           gsc[:, 65:66], 0.0, ALU.is_equal, ALU.add))
                kb.dve.wait(v)
                kb.act.wait(v, act0 + t + 1)
                kb.dve.inc(v.scalar_tensor_tensor(gsc[:, 48:64], gsc[:, 16:32], 1.0,
                                                  logit[:, :], ALU.mult, ALU.mult,
                                                  accum_out=gsc[:, 66:67]))
                kb.dve.inc(v.scalar_tensor_tensor(kb.junkv[:, 0:16], gsc[:, 32:48], 1.0,
                                                  logit[:, :], ALU.mult, ALU.mult,
                                                  accum_out=gsc[:, 67:68]))
                kb.dve.inc(v.tensor_copy(kb.junkv[:, 16:32], gsc[:, 0:16]))
                kb.dve.wait(v)
                kb.dve.inc(v.tensor_tensor(gsc[:, 68:69], gsc[:, 66:67],
                                           gsc[:, 67:68], ALU.add))
                kb.dve.wait(v)
                kb.dve.inc(v.tensor_scalar(gsc[:, 68:69], gsc[:, 68:69], 1e-10,
                                           None, ALU.add))
                kb.dve.wait(v)
                kb.dve.inc(v.reciprocal(gsc[:, 69:70], gsc[:, 68:69]))
                kb.dve.wait(v)
                kb.dve.inc(v.tensor_tensor(gsc[:, 66:67], gsc[:, 66:67],
                                           gsc[:, 69:70], ALU.mult))
                kb.dve.inc(v.tensor_tensor(gsc[:, 67:68], gsc[:, 67:68],
                                           gsc[:, 69:70], ALU.mult))
                kb.dve.wait(v)
                kb.dve.inc(v.tensor_scalar(gsc[:, 48:64], gsc[:, 16:32],
                                           gsc[:, 66:67], None, ALU.mult))
                kb.dve.wait(v)
                kb.dve.inc(v.scalar_tensor_tensor(
                    gatew[:, t * 16:(t + 1) * 16], gsc[:, 32:48],
                    gsc[:, 67:68], gsc[:, 48:64], ALU.mult, ALU.add))
        @blk.scalar
        def _(sc):
            for t in range(NT):
                kb.dve.wait(sc, dve0 + t * NDV + 2)
                if t > 0:
                    kb.act.wait(sc)
                kb.act.inc(sc.activation(logit[:, :], logit[:, :], AF.Sigmoid))


def emit_expert_core(kb, name, lnT, gatew, gatew_col, wbuf_, b2r_, hs, h_sb,
                     ph, py, dff, bcol0, wslot=None, prefetch=None):
    nc = kb.nc
    nm = dff // 128
    w1o, w3o, w2o = 0, 8 * dff, 16 * dff
    pe0, dve0, act0 = kb.pe.v, kb.dve.v, kb.act.v
    x_res = kb.x_res
    def pe_h(ch, m):
        base = ch * (nm + 4)
        return pe0 + base + (m + 1 if m < 2 else m + 5 if ch > 0 else m + 1)
    def pe_y(ch, tt):
        return pe0 + (ch + 1) * (nm + 4) - (nm - 2) - 4 + tt + 1 if False else                pe0 + (ch + 1) * (nm + 4) + 2 + tt + 1 - (nm + 4) + nm - 2 + 0
    sched = []
    for ch in range(5):
        if ch < 4:
            for m in range(nm):
                if ch > 0 and m == 2:
                    for tt in range(4):
                        sched.append(("y", ch - 1, tt))
                sched.append(("h", ch, m))
        else:
            for tt in range(4):
                sched.append(("y", 3, tt))
    pe_idx = {ev: i + 1 for i, ev in enumerate(sched)}
    act_idx = {}
    dve_idx = {}
    di_ = 0
    for ch in range(4):
        for m in range(nm):
            dve_idx[("c", ch, m)] = (di_ := di_ + 1)
        for tt in range(4):
            dve_idx[("e", ch, tt)] = (di_ := di_ + 1)
    ai_ = 0
    for ch in range(4):
        for m in range(nm):
            act_idx[(ch, m)] = (ai_ := ai_ + 1)

    with nc.Block(name) as blk:
        if prefetch is not None:
            @blk.gpsimd
            def _(g):
                prefetch(g)
        @blk.tensor
        def _(t_):
            if wslot is not None:
                wslot[0].wait(t_, wslot[1])
            for ev in sched:
                if ev[0] == "h":
                    _, ch, m = ev
                    gm = ch * nm + m
                    if gm >= 2:
                        pch, pm = (ch, m - 2) if m >= 2 else (ch - 1, nm - 2 + m)
                        kb.act.wait(t_, act0 + act_idx[(pch, pm)])
                        kb.dve.wait(t_, dve0 + dve_idx[("c", pch, pm)])
                    tok0 = ch * 512
                    for k in range(8):
                        t_.matmul(ph[m % 2][:, :],
                                  wbuf_[:, w1o + k * dff + m * 128:w1o + k * dff + (m + 1) * 128],
                                  lnT[:, k * TPC + tok0:k * TPC + tok0 + 512],
                                  start=(k == 0), stop=(k == 7))
                    for k in range(8):
                        ins = t_.matmul(ph[2 + m % 2][:, :],
                                        wbuf_[:, w3o + k * dff + m * 128:w3o + k * dff + (m + 1) * 128],
                                        lnT[:, k * TPC + tok0:k * TPC + tok0 + 512],
                                        start=(k == 0), stop=(k == 7))
                    kb.pe.inc(ins)
                else:
                    _, ch, tt = ev
                    gt = ch * 4 + tt
                    kb.dve.wait(t_, dve0 + dve_idx[("c", ch, nm - 1)])
                    if gt >= 2:
                        kb.dve.wait(t_, dve0 + dve_idx[("e", (gt - 2) // 4, (gt - 2) % 4)])
                    pyb = py[tt % 2]
                    hoff = (ch % 2) * nm * 512
                    for k in range(nm):
                        for hf in range(2):
                            t_.matmul(pyb[:, hf * 512:hf * 512 + 512],
                                      h_sb[:, hoff + k * 512 + tt * 128:hoff + k * 512 + tt * 128 + 128],
                                      wbuf_[:, w2o + k * 1024 + hf * 512:w2o + k * 1024 + hf * 512 + 512],
                                      start=(k == 0), stop=False)
                    for hf in range(2):
                        ins = t_.matmul(pyb[:, hf * 512:hf * 512 + 512],
                                        kb.ones_row[:, :], b2r_[:, hf * 512:hf * 512 + 512],
                                        start=False, stop=True)
                    kb.pe.inc(ins)

        @blk.scalar
        def _(sc):
            for ch in range(4):
                for m in range(nm):
                    kb.pe.wait(sc, pe0 + pe_idx[("h", ch, m)])
                    if ch * nm + m >= 2:
                        pch, pm = (ch, m - 2) if m >= 2 else (ch - 1, nm - 2 + m)
                        kb.act.wait(sc)
                        kb.dve.wait(sc, dve0 + dve_idx[("c", pch, pm)])
                    kb.act.inc(sc.activation(hs[:, (m % 2) * 512:(m % 2) * 512 + 512],
                                             ph[m % 2][:, :], AF.Silu,
                                             bias=kb.b1cols[:, bcol0 + m:bcol0 + m + 1]))

        @blk.vector
        def _(v):
            for ch in range(4):
                if ch >= 2:
                    kb.dve.wait(v)
                    kb.pe.wait(v, pe0 + pe_idx[("y", ch - 2, 3)])
                for m in range(nm):
                    kb.act.wait(v, act0 + act_idx[(ch, m)])
                    kb.dve.inc(v.scalar_tensor_tensor(
                        h_sb[:, (ch % 2) * nm * 512 + m * 512:(ch % 2) * nm * 512 + (m + 1) * 512],
                        ph[2 + m % 2][:, :],
                        kb.b3cols[:, bcol0 + m:bcol0 + m + 1],
                        hs[:, (m % 2) * 512:(m % 2) * 512 + 512],
                        ALU.add, ALU.mult))
                for tt in range(4):
                    gt = ch * 4 + tt
                    kb.pe.wait(v, pe0 + pe_idx[("y", ch, tt)])
                    pyb = py[tt % 2]
                    if gatew_col is None:
                        ins = v.tensor_tensor(
                            x_res[:, gt * D:(gt + 1) * D],
                            x_res[:, gt * D:(gt + 1) * D], pyb[:, :], ALU.add)
                    else:
                        ins = v.scalar_tensor_tensor(
                            x_res[:, gt * D:(gt + 1) * D],
                            pyb[:, :],
                            gatew[:, gt * 16 + gatew_col:gt * 16 + gatew_col + 1],
                            x_res[:, gt * D:(gt + 1) * D],
                            ALU.mult, ALU.add)
                    kb.dve.inc(ins)



def build_attn_b(kb, di, dscr):
    nc = kb.nc
    cc_in, cc_out = dscr["cc_gla_in"], dscr["cc_gla_out"]
    prefix_d = dscr["gla_prefix"]
    kp_d, el_d, o_d = dscr["gla_kp"], dscr["gla_el"], dscr["gla_o"]

    with contextlib.ExitStack() as pctx:
        lnT = kb.sb(pctx, "gb_lnT", [128, 8 * TPC], BF16)
        ln_tm = kb.sb(pctx, "gb_lntm", [128, 2 * D], BF16)
        with contextlib.ExitStack() as qctx:
            pst = [kb.ps(qctx, f"gb_pst{i}", [128, 1024], BF16) for i in range(2)]
            emit_ln(kb, "gb", ln_tm, lnT=lnT, pst=pst)

        v_tm = kb.sb(pctx, "gb_v", [128, NT * D], BF16)
        t1 = kb.sb(pctx, "gb_t1", [64, TPC], BF16)

        with contextlib.ExitStack() as qctx:
            wvb = kb.sb(qctx, "gb_wvb", [128, 8 * 1024], BF16)
            wg1s = kb.sb(qctx, "gb_wg1", [128, 8 * 64], BF16)
            pv = [kb.ps(qctx, f"gb_pv{i}", [128, 1024], F32) for i in range(2)]
            pg1 = kb.ps(qctx, "gb_pg1", [64, 2048], F32)
            with nc.Block("gb_a0w") as blk:
                @blk.gpsimd
                def _(g):
                    kb.gx.inc(g.dma_start(
                        wvb.ap().rearrange("p (k n) -> p k n", k=8),
                        di["wv_b"].ap().rearrange("(k p) n -> p k n", p=128)), 16)
                    kb.gx.inc(g.dma_start(
                        wg1s.ap().rearrange("p (k n) -> p k n", k=8),
                        di["wg1"].ap().rearrange("(k p) n -> p k n", p=128)), 16)
                    kb.gx.wait(g)
            pe0, dve0 = kb.pe.v, kb.dve.v
            with nc.Block("gb_a0") as blk:
                @blk.tensor
                def _(t_, pe0=pe0, dve0=dve0):
                    for t in range(NT):
                        if t >= 2:
                            kb.dve.wait(t_, dve0 + t - 1)
                        for k in range(8):
                            for hf in range(2):
                                ins = t_.matmul(pv[t % 2][:, hf * 512:hf * 512 + 512],
                                                lnT[:, k * TPC + t * 128:k * TPC + (t + 1) * 128],
                                                wvb[:, k * 1024 + hf * 512:k * 1024 + hf * 512 + 512],
                                                start=(k == 0), stop=(k == 7))
                        kb.pe.inc(ins)
                    for k in range(8):
                        for q4 in range(4):
                            ins = t_.matmul(pg1[:, q4 * 512:q4 * 512 + 512],
                                            wg1s[:, k * 64:(k + 1) * 64],
                                            lnT[:, k * TPC + q4 * 512:k * TPC + q4 * 512 + 512],
                                            start=(k == 0), stop=(k == 7))
                    kb.pe.inc(ins)
                @blk.vector
                def _(v, pe0=pe0):
                    for t in range(NT):
                        kb.pe.wait(v, pe0 + t + 1)
                        kb.dve.inc(v.tensor_copy(v_tm[:, t * D:(t + 1) * D],
                                                 pv[t % 2][:, :]))
                    kb.pe.wait(v, pe0 + NT + 1)
                    kb.dve.inc(v.tensor_copy(t1[:, :], pg1[:, :]))

        with contextlib.ExitStack() as qctx:
            wkh = kb.sb(qctx, "gb_wkh", [128, 2 * 8 * 128], BF16)
            csh = kb.sb(qctx, "gb_csh", [128, 2 * TPC], BF16)
            ekh = kb.sb(qctx, "gb_ekh", [128, TPC], F32)
            elh = kb.sb(qctx, "gb_elh", [128, TPC], F32)
            kph = kb.sb(qctx, "gb_kph", [128, TPC], BF16)
            ktm = kb.sb(qctx, "gb_ktm", [128, 128], BF16)
            stage = kb.sb(qctx, "gb_stage", [128, 2 * H * 65], F32)
            pkh = kb.ps(qctx, "gb_pkh", [128, 2048], F32)
            pstate = [kb.ps(qctx, f"gb_pstt{i}", [128, 64], F32) for i in range(2)]
            ptr = [kb.ps(qctx, f"gb_ptr{i}", [128, 128], BF16) for i in range(2)]

            with nc.Block("gb_a1pre") as blk:
                @blk.gpsimd
                def _(g):
                    kb.g0.inc(g.dma_start(
                        wkh.ap().rearrange("p (sx x) -> p sx x", sx=2)[:, 0, :].rearrange(
                            "p (k n) -> p k n", k=8),
                        di["wk_b_dup"].ap()[:, 0:128].rearrange("(k p) n -> p k n", p=128)), 16)
                    kb.g0.inc(g.dma_start(csh[:, 0:TPC], di["ropecs"].ap()[0]), 16)
                    kb.g1.inc(g.dma_start(
                        wkh.ap().rearrange("p (sx x) -> p sx x", sx=2)[:, 1, :].rearrange(
                            "p (k n) -> p k n", k=8),
                        di["wk_b_dup"].ap()[:, 128:256].rearrange("(k p) n -> p k n", p=128)), 16)
                    kb.g1.inc(g.dma_start(csh[:, TPC:2 * TPC], di["ropecs"].ap()[1]), 16)

            emit_gla_a1_merged(kb, di, lnT, v_tm, wkh, csh, ekh, elh, kph, ktm,
                               stage, pkh, pstate, ptr, kp_d, el_d)

            with nc.Block("gb_ship") as blk:
                @blk.sync
                def _(s):
                    kb.sx.inc(s.dma_start(
                        cc_in.ap().rearrange("(xx p) c -> p xx c", p=128),
                        stage.ap().rearrange("p (xx c) -> p xx c", c=65)), 16)
                    kb.sx.wait(s)
                @blk.gpsimd
                def _(g):
                    kb.sx.wait(g)
                    kb.cc.inc(g.collective_compute(
                        "AllGather", ALU.bypass, replica_groups=[list(range(8))],
                        ins=[cc_in.ap().opt()], outs=[cc_out.ap().opt()]))
                    kb.cc.wait(g)

        sin_sb = kb.sb(pctx, "gb_sin", [128, 2 * 1040], F32)
        einp1 = kb.sb(pctx, "gb_einp1", [128, 32], F32)
        with contextlib.ExitStack() as qctx:
            acc = kb.sb(qctx, "gb_acc", [128, 1040], F32)
            sts = kb.sb(qctx, "gb_sts", [128, 1040], F32)
            emit_gla_prefix(kb, cc_out, prefix_d, acc, sts, sin_sb, einp1)

        with contextlib.ExitStack() as qctx:
            wqh = kb.sb(qctx, "gb_wqh", [128, 2 * 8 * 128], BF16)
            csh2 = kb.sb(qctx, "gb_csh2", [128, 2 * TPC], BF16)
            kph2 = kb.sb(qctx, "gb_kph2", [128, 2 * TPC], BF16)
            elh2 = kb.sb(qctx, "gb_elh2", [128, 2 * TPC], BF16)
            sqh = kb.sb(qctx, "gb_sqh", [128, TPC], BF16)
            invh = kb.sb(qctx, "gb_invh", [128, TPC], F32)
            qph = kb.sb(qctx, "gb_qph", [128, TPC], BF16)
            ktm2 = kb.sb(qctx, "gb_ktm2", [128, 128], BF16)
            psb = kb.sb(qctx, "gb_psb", [128, 128], BF16)
            srun = kb.sb(qctx, "gb_srun", [128, 128], F32)
            srbf = kb.sb(qctx, "gb_srbf", [128, 128], BF16)
            osh = kb.sb(qctx, "gb_osh", [128, NT * 64], BF16)
            pqh = kb.ps(qctx, "gb_pqh", [128, 1024], F32)
            ptr2 = [kb.ps(qctx, f"gb_ptr2{i}", [128, 128], BF16) for i in range(2)]
            psc = [kb.ps(qctx, f"gb_psc{i}", [128, 128], F32) for i in range(2)]
            pou = [kb.ps(qctx, f"gb_pou{i}", [128, 128], F32) for i in range(2)]
            emit_gla_b_merged(kb, di, lnT, v_tm, sin_sb, einp1,
                              wqh, csh2, kph2, elh2, sqh, invh, qph, ktm2, psb,
                              srun, srbf, osh, pqh, ptr2, psc, pou,
                              kp_d, el_d, o_d)

        with contextlib.ExitStack() as qctx:
            emit_gla_c(kb, qctx, di, t1, o_d)


def emit_gla_a1_merged(kb, di, lnT, v_tm, wkh, csh, ekh, elh, kph, ktm, stage,
                       pkh, pstate, ptr, kp_d, el_d):
    nc = kb.nc
    pe0, dve0, act0, gx0 = kb.pe.v, kb.dve.v, kb.act.v, kb.gx.v
    g0v, g1v = kb.g0.v, kb.g1.v
    DVE_P = 21

    def wtgt(h):
        return (g0v if h % 2 == 0 else g1v) + 32 * (h // 2)

    def kpv(h):
        return pe0 + (h + 1 if h < 2 else 33 * h - 31)

    def S(h):
        return pe0 + (2 + 33 * h if h <= 14 else 496)

    def chT(h, gt):
        return S(h) + 1 + 2 * gt

    def chS(h, gt):
        return S(h) + 2 + 2 * gt

    views = {}
    for h in range(H):
        i = h % 2
        views[h] = dict(
            ws=wkh.ap().rearrange("p (sx x) -> p sx x", sx=2)[:, i, :],
            cs=csh.ap().rearrange("p (sx x) -> p sx x", sx=2)[:, i, :])

    with nc.Block("gb_a1") as blk:
        @blk.tensor
        def _(t_):
            def emit_kp(h):
                vw = views[h]
                gsl = kb.g0 if h % 2 == 0 else kb.g1
                gsl.wait(t_, wtgt(h))
                if h >= 1:
                    kb.act.wait(t_, act0 + h)
                ins = None
                for k in range(8):
                    for q4 in range(4):
                        ins = t_.matmul(pkh[:, q4 * 512:q4 * 512 + 512],
                                        vw["ws"][:, k * 128:(k + 1) * 128],
                                        lnT[:, k * TPC + q4 * 512:k * TPC + q4 * 512 + 512],
                                        start=(k == 0), stop=(k == 7))
                kb.pe.inc(ins)

            def emit_ch(h):
                dvb = dve0 + h * DVE_P
                for gt in range(NT):
                    sl, cchunk = gt // 8, gt % 8
                    kb.dve.wait(t_, dvb + 3 + max(0, gt - 1))
                    kb.pe.inc(t_.transpose(ptr[gt % 2][:, :],
                                           kph[:, gt * 128:(gt + 1) * 128],
                                           kb.ident[:, :]))
                    kb.dve.wait(t_, dvb + 4 + gt)
                    kb.pe.inc(t_.matmul(pstate[sl][:, :], ktm[:, :],
                                        v_tm[:, gt * D + h * 64:gt * D + h * 64 + 64],
                                        start=(cchunk == 0), stop=(cchunk == 7)))

            emit_kp(0)
            emit_kp(1)
            for h in range(H):
                emit_ch(h)
                if h + 2 < H:
                    emit_kp(h + 2)

        @blk.scalar
        def _(sc):
            for h in range(H):
                kb.pe.wait(sc, kpv(h))
                if h >= 1:
                    kb.dve.wait(sc, dve0 + (h - 1) * DVE_P + 3)
                kb.act.inc(sc.activation(ekh[:, :], pkh[:, :], AF.Exp))

        @blk.vector
        def _(v):
            for h in range(H):
                vw = views[h]
                dvb = dve0 + h * DVE_P
                kb.gx.wait(v, gx0 + 32 * h)
                kb.act.wait(v, act0 + h + 1)
                if h >= 1:
                    kb.dve.wait(v)
                kb.dve.inc(v.tensor_tensor_scan(elh[:, 0:SLAB], kb.ones_f[:, :],
                                                ekh[:, 0:SLAB], 0.0, ALU.mult, ALU.add))
                kb.dve.inc(v.tensor_tensor_scan(elh[:, SLAB:TPC], kb.ones_f[:, :],
                                                ekh[:, SLAB:TPC], 0.0, ALU.mult, ALU.add))
                if h >= 1:
                    kb.pe.wait(v, chT(h - 1, NT - 1))
                kb.dve.inc(v.tensor_tensor(kph[:, :], ekh[:, :], vw["cs"], ALU.mult))
                for gt in range(NT):
                    kb.pe.wait(v, chT(h, gt))
                    kb.dve.inc(v.tensor_copy(ktm[:, :], ptr[gt % 2][:, :]))
                kb.pe.wait(v, chS(h, NT - 1))
                for sl in range(2):
                    v.tensor_copy(stage[:, (sl * H + h) * 65:(sl * H + h) * 65 + 64],
                                  pstate[sl][:, :])
                    kb.dve.inc(v.tensor_copy(
                        stage[:, (sl * H + h) * 65 + 64:(sl * H + h) * 65 + 65],
                        elh[:, (sl + 1) * SLAB - 1:(sl + 1) * SLAB]))

        @blk.gpsimd
        def _(g):
            for h in range(H):
                dvb = dve0 + h * DVE_P
                kb.dve.wait(g, dvb + 3)
                kb.gx.inc(g.dma_start(kp_d.ap()[h * 128:(h + 1) * 128, :], kph[:, :]), 16)
                kb.gx.inc(g.dma_start(el_d.ap()[h * 128:(h + 1) * 128, :], elh[:, :]), 16)
                if h + 2 < H:
                    hn = h + 2
                    vw = views[hn]
                    gsl = kb.g0 if hn % 2 == 0 else kb.g1
                    kb.pe.wait(g, kpv(h))
                    gsl.inc(g.dma_start(
                        vw["ws"].rearrange("p (k n) -> p k n", k=8),
                        di["wk_b_dup"].ap()[:, hn * 128:(hn + 1) * 128].rearrange(
                            "(k p) n -> p k n", p=128)), 16)
                    gsl.inc(g.dma_start(vw["cs"], di["ropecs"].ap()[hn]), 16)


def emit_gla_prefix(kb, cc_out, prefix_d, acc, sts, sin_sb, einp1):
    nc = kb.nc
    dve0, gx0 = kb.dve.v, kb.gx.v
    with nc.Block("gb_prefix") as blk:
        @blk.gpsimd
        def _(g):
            if True:
                ro = kb.ro
                for s_idx in range(NSLAB):
                    kb.dve.wait(g, dve0 + 1 + s_idx)
                    kb.gx.inc(g.dma_start(
                        prefix_d.ap()[s_idx * 128:(s_idx + 1) * 128, :], acc[:, :]), 16)
                    g.reg_load(ro, kb.offs[0:1, 16 + s_idx:17 + s_idx])
                    kb.gx.inc(g.dma_start(
                        sts.ap().rearrange("p (hh c) -> p hh c", c=65),
                        bass.AP(cc_out, ro, [[65, 128], [65 * 128, 16], [1, 65]])), 16)
                kb.dve.wait(g, dve0 + 1 + NSLAB)
                for sl in range(2):
                    g.reg_load(ro, kb.offs[0:1, 24 + sl:25 + sl])
                    kb.gx.inc(g.dma_start(
                        sin_sb[:, sl * 1040:(sl + 1) * 1040],
                        bass.AP(prefix_d, ro, [[1040, 128], [1, 1040]])), 16)
        @blk.vector
        def _(v):
            kb.dve.inc(v.memset(acc[:, :], 0.0))
            for s_idx in range(NSLAB):
                kb.gx.wait(v, gx0 + 16 * (2 * s_idx + 2))
                kb.dve.inc(v.tensor_tensor(acc[:, :], acc[:, :], sts[:, :], ALU.add))
            kb.gx.wait(v, gx0 + 16 * (2 * NSLAB + 2))
            for sl in range(2):
                sview = sin_sb.ap().rearrange("p (sx x) -> p sx x", sx=2)[
                    :, sl, :].rearrange("p (hh c) -> p hh c", c=65)[:, :, 64:65]
                kb.dve.inc(v.tensor_scalar(
                    einp1[:, sl * 16:sl * 16 + 16],
                    sview.rearrange("p hh c -> p (hh c)"),
                    1.0, None, ALU.add))


def emit_gla_b_merged(kb, di, lnT, v_tm, sin_sb, einp1, wqh, csh, kph, elh,
                      sqh, invh, qph, ktm, psb, srun, srbf, osh, pqh, ptr, psc,
                      pou, kp_d, el_d, o_d):
    nc = kb.nc
    pe0, dve0, act0, gx0 = kb.pe.v, kb.dve.v, kb.act.v, kb.gx.v
    PE_P, ACT_P, DVE_P = 50, 6, 84
    gsl_t = {}
    gsl_t[0] = kb.g0.v + 64
    gsl_t[1] = kb.g1.v + 64
    views = {}
    for h in range(H):
        i = h % 2
        views[h] = dict(
            ws=wqh.ap().rearrange("p (sx x) -> p sx x", sx=2)[:, i, :],
            cs=csh.ap().rearrange("p (sx x) -> p sx x", sx=2)[:, i, :],
            kpv=kph.ap().rearrange("p (sx x) -> p sx x", sx=2)[:, i, :],
            elv=elh.ap().rearrange("p (sx x) -> p sx x", sx=2)[:, i, :])

    with nc.Block("gb_b") as blk:
        @blk.gpsimd
        def _(g):
            for h in range(2):
                vw = views[h]
                gsl = kb.g0 if h % 2 == 0 else kb.g1
                gsl.inc(g.dma_start(
                    vw["ws"].rearrange("p (k n) -> p k n", k=8),
                    di["wq_b_dup"].ap()[:, h * 128:(h + 1) * 128].rearrange(
                        "(k p) n -> p k n", p=128)), 16)
                gsl.inc(g.dma_start(vw["cs"], di["ropecs"].ap()[h]), 16)
                gsl.inc(g.dma_start(vw["kpv"], kp_d.ap()[h * 128:(h + 1) * 128, :]), 16)
                gsl.inc(g.dma_start(vw["elv"], el_d.ap()[h * 128:(h + 1) * 128, :]), 16)
            for h in range(H):
                kb.dve.wait(g, dve0 + (h + 1) * DVE_P)
                kb.gx.inc(g.dma_start(
                    o_d.ap().rearrange("(t p) c -> p t c", p=128)[:, :, h * 64:(h + 1) * 64],
                    osh.ap().rearrange("p (t c) -> p t c", c=64)), 16)
                if h + 2 < H:
                    hn = h + 2
                    vw = views[hn]
                    gsl = kb.g0 if hn % 2 == 0 else kb.g1
                    kb.pe.wait(g, pe0 + (h + 1) * PE_P)
                    gsl.inc(g.dma_start(
                        vw["ws"].rearrange("p (k n) -> p k n", k=8),
                        di["wq_b_dup"].ap()[:, hn * 128:(hn + 1) * 128].rearrange(
                            "(k p) n -> p k n", p=128)), 16)
                    gsl.inc(g.dma_start(vw["cs"], di["ropecs"].ap()[hn]), 16)
                    gsl.inc(g.dma_start(vw["kpv"], kp_d.ap()[hn * 128:(hn + 1) * 128, :]), 16)
                    gsl.inc(g.dma_start(vw["elv"], el_d.ap()[hn * 128:(hn + 1) * 128, :]), 16)
                    gsl_t[hn] = gsl.v

        @blk.tensor
        def _(t_):
            for h in range(H):
                vw = views[h]
                pe_b = pe0 + h * PE_P
                dve_b = dve0 + h * DVE_P
                act_b = act0 + h * ACT_P
                gsl = kb.g0 if h % 2 == 0 else kb.g1
                gsl.wait(t_, gsl_t[h])
                if h > 0:
                    kb.act.wait(t_, act_b - ACT_P + 2)
                ins = None
                for hq in range(2):
                    if hq == 1:
                        kb.act.wait(t_, act_b + 1)
                    for k in range(8):
                        for q4 in range(2):
                            ins = t_.matmul(pqh[:, q4 * 512:q4 * 512 + 512],
                                            vw["ws"][:, k * 128:(k + 1) * 128],
                                            lnT[:, k * TPC + hq * 1024 + q4 * 512:k * TPC + hq * 1024 + q4 * 512 + 512],
                                            start=(k == 0), stop=(k == 7))
                    kb.pe.inc(ins)
                for i, (sl, cchunk) in enumerate(ORDER_B):
                    gt = sl * 8 + cchunk
                    if i >= 2:
                        kb.dve.wait(t_, dve_b + 4 + 5 * (i - 2) + 1)
                    elif h > 0:
                        kb.dve.wait(t_, dve_b - DVE_P + 4 + 5 * (14 + i) + 1)
                    kb.pe.inc(t_.transpose(ptr[i % 2][:, :], vw["kpv"][:, gt * 128:(gt + 1) * 128],
                                           kb.ident[:, :]))
                    kb.dve.wait(t_, dve_b + 2)
                    if i >= 2:
                        kb.dve.wait(t_, dve_b + 4 + 5 * (i - 2) + 2)
                    elif h > 0:
                        kb.dve.wait(t_, dve_b - DVE_P + 4 + 5 * (14 + i) + 2)
                    kb.pe.inc(t_.matmul(psc[i % 2][:, :], vw["kpv"][:, gt * 128:(gt + 1) * 128],
                                        qph[:, gt * 128:(gt + 1) * 128], start=True, stop=True))
                    kb.dve.wait(t_, dve_b + 4 + 5 * i + 2)
                    t_.matmul(pou[i % 2][:, 0:64], psb[:, :],
                              v_tm[:, gt * D + h * 64:gt * D + h * 64 + 64],
                              start=True, stop=False)
                    t_.matmul(pou[i % 2][:, 0:64], qph[:, gt * 128:(gt + 1) * 128],
                              srbf[:, sl * 64:sl * 64 + 64], start=False, stop=True)
                    kb.pe.inc(t_.matmul(pou[i % 2][:, 64:128], ktm[:, :],
                                        v_tm[:, gt * D + h * 64:gt * D + h * 64 + 64],
                                        start=True, stop=True))

        @blk.scalar
        def _(sc):
            for h in range(H):
                vw = views[h]
                pe_b = pe0 + h * PE_P
                dve_b = dve0 + h * DVE_P
                kb.pe.wait(sc, pe_b + 1)
                if h > 0:
                    kb.dve.wait(sc, dve_b - DVE_P + 2)
                    kb.act.wait(sc)
                kb.act.inc(sc.activation(sqh[:, 0:1024], pqh[:, :], AF.Silu))
                kb.pe.wait(sc, pe_b + 2)
                kb.act.inc(sc.activation(sqh[:, 1024:2048], pqh[:, :], AF.Silu))
                for sl in range(2):
                    kb.act.inc(sc.activation(invh[:, sl * SLAB:(sl + 1) * SLAB],
                                             vw["elv"][:, sl * SLAB:(sl + 1) * SLAB], AF.Ln,
                                             bias=einp1[:, sl * 16 + h:sl * 16 + h + 1]))
                    kb.act.wait(sc)
                    kb.act.inc(sc.activation(invh[:, sl * SLAB:(sl + 1) * SLAB],
                                             invh[:, sl * SLAB:(sl + 1) * SLAB], AF.Exp,
                                             scale=f32(-1.0)))

        @blk.vector
        def _(v):
            for h in range(H):
                vw = views[h]
                pe_b = pe0 + h * PE_P
                act_b = act0 + h * ACT_P
                kb.gx.wait(v, gx0 + h * 16)
                kb.act.wait(v, act_b + 6)
                if h > 0:
                    kb.dve.wait(v)
                kb.dve.inc(v.tensor_tensor(invh[:, :], invh[:, :], sqh[:, :], ALU.mult))
                kb.dve.wait(v)
                if h > 0:
                    kb.pe.wait(v, pe_b)
                kb.dve.inc(v.tensor_tensor(qph[:, :], invh[:, :], vw["cs"], ALU.mult))
                kb.dve.wait(v)
                for sl in range(2):
                    kb.dve.inc(v.tensor_copy(
                        srbf[:, sl * 64:sl * 64 + 64],
                        sin_sb[:, sl * 1040 + h * 65:sl * 1040 + h * 65 + 64]))
                    v.tensor_copy(srun[:, sl * 64:sl * 64 + 64],
                                  sin_sb[:, sl * 1040 + h * 65:sl * 1040 + h * 65 + 64])
                for i, (sl, cchunk) in enumerate(ORDER_B):
                    gt = sl * 8 + cchunk
                    kb.pe.wait(v, pe_b + 3 + 3 * i)
                    kb.dve.inc(v.tensor_copy(ktm[:, :], ptr[i % 2][:, :]))
                    kb.pe.wait(v, pe_b + 4 + 3 * i)
                    kb.dve.inc(v.tensor_tensor(psb[:, :], psc[i % 2][:, :],
                                               kb.trimask[:, :], ALU.mult))
                    kb.pe.wait(v, pe_b + 5 + 3 * i)
                    kb.dve.inc(v.tensor_tensor(srun[:, sl * 64:sl * 64 + 64],
                                               srun[:, sl * 64:sl * 64 + 64],
                                               pou[i % 2][:, 64:128], ALU.add))
                    kb.dve.wait(v)
                    kb.dve.inc(v.tensor_copy(srbf[:, sl * 64:sl * 64 + 64],
                                             srun[:, sl * 64:sl * 64 + 64]))
                    kb.dve.inc(v.tensor_copy(osh[:, gt * 64:(gt + 1) * 64],
                                             pou[i % 2][:, 0:64]))


def emit_gla_c(kb, qctx, di, t1, o_d):
    nc = kb.nc
    wg2s = kb.sb(qctx, "gb_wg2", [64, 1024], BF16)
    wob = kb.sb(qctx, "gb_wob", [128, 8 * 1024], BF16)
    ot = kb.sb(qctx, "gb_ot", [128, 2048], BF16)
    gg = kb.sb(qctx, "gb_gg", [128, 1024], BF16)
    gsig = kb.sb(qctx, "gb_gsig", [128, 1024], BF16)
    nrm = kb.sb(qctx, "gb_nrm", [128, 1024], BF16)
    nfm = kb.sb(qctx, "gb_nfm", [128, 1024], BF16)
    rst = kb.sb(qctx, "gb_rst", [128, 48], F32)
    pg2 = kb.ps(qctx, "gb_pg2", [128, 1024], F32)
    pnt = kb.ps(qctx, "gb_pnt", [128, 1024], BF16)
    pyo = kb.ps(qctx, "gb_pyo", [128, 1024], F32)

    with nc.Block("gb_cw") as blk:
        @blk.gpsimd
        def _(g):
            kb.gx.inc(g.dma_start(wg2s[:, :], di["wg2"].ap()), 16)
            kb.gx.inc(g.dma_start(
                wob.ap().rearrange("p (k n) -> p k n", k=8),
                di["wo_b"].ap().rearrange("(k p) n -> p k n", p=128)), 16)
            kb.gx.wait(g)
        @blk.vector
        def _(v):
            kb.gx.wait(v)
            for k in range(8):
                v.tensor_scalar(wob[:, k * 1024:(k + 1) * 1024],
                                wob[:, k * 1024:(k + 1) * 1024],
                                kb.nw[:, k:k + 1], None, ALU.mult)

    pe0, dve0, act0, gx0 = kb.pe.v, kb.dve.v, kb.act.v, kb.gx.v
    NDV = 21
    with nc.Block("gb_c") as blk:
        @blk.gpsimd
        def _(g):
            for t in range(NT):
                if t > 0:
                    kb.gx.wait(g, gx0 + t * 16)
                if t >= 2:
                    kb.dve.wait(g, dve0 + (t - 2) * NDV + 1)
                kb.gx.inc(g.dma_start(
                    ot[:, (t % 2) * 1024:(t % 2) * 1024 + 1024],
                    o_d.ap()[t * 128:(t + 1) * 128, :]), 16)
        @blk.tensor
        def _(t_):
            for t in range(NT):
                if t > 0:
                    kb.act.wait(t_, act0 + (t - 1) * 3 + 1)
                for hf in range(2):
                    ins = t_.matmul(pg2[:, hf * 512:hf * 512 + 512],
                                    t1[:, t * 128:(t + 1) * 128],
                                    wg2s[:, hf * 512:hf * 512 + 512],
                                    start=True, stop=True)
                kb.pe.inc(ins)
                kb.dve.wait(t_, dve0 + t * NDV + 19)
                if t > 0:
                    kb.dve.wait(t_, dve0 + (t - 1) * NDV + 20)
                for hp in range(8):
                    ins = t_.transpose(pnt[:, hp * 128:(hp + 1) * 128],
                                       nrm[:, hp * 128:(hp + 1) * 128],
                                       kb.ident[:, :])
                kb.pe.inc(ins)
                kb.dve.wait(t_, dve0 + t * NDV + 20)
                if t > 0:
                    kb.dve.wait(t_, dve0 + (t - 1) * NDV + 21)
                for k in range(8):
                    for hf in range(2):
                        ins = t_.matmul(
                            pyo[:, hf * 512:hf * 512 + 512],
                            nfm[:, k * 128:(k + 1) * 128],
                            wob[:, k * 1024 + hf * 512:k * 1024 + hf * 512 + 512],
                            start=(k == 0), stop=(k == 7))
                kb.pe.inc(ins)
        @blk.scalar
        def _(sc):
            for t in range(NT):
                kb.pe.wait(sc, pe0 + t * 3 + 1)
                if t > 0:
                    kb.dve.wait(sc, dve0 + (t - 1) * NDV + 1)
                    kb.act.wait(sc)
                kb.act.inc(sc.activation(gsig[:, :], pg2[:, :], AF.Sigmoid))
                kb.dve.wait(sc, dve0 + t * NDV + 18)
                kb.act.inc(sc.activation(rst[:, 16:32], rst[:, 0:16], AF.Ln,
                                         scale=f32(1.0 / HD), bias=f32(EPS_RMS)))
                kb.act.wait(sc)
                kb.act.inc(sc.activation(rst[:, 32:48], rst[:, 16:32], AF.Exp,
                                         scale=f32(-0.5)))
        @blk.vector
        def _(v):
            for t in range(NT):
                kb.act.wait(v, act0 + t * 3 + 1)
                kb.gx.wait(v, gx0 + (t + 1) * 16)
                if t > 0:
                    kb.dve.wait(v)
                kb.dve.inc(v.tensor_tensor(gg[:, :],
                                           ot[:, (t % 2) * 1024:(t % 2) * 1024 + 1024],
                                           gsig[:, :], ALU.mult))
                kb.dve.wait(v)
                for h in range(16):
                    kb.dve.inc(v.scalar_tensor_tensor(
                        kb.junk[:, h * 64:(h + 1) * 64], gg[:, h * 64:(h + 1) * 64], 1.0,
                        gg[:, h * 64:(h + 1) * 64], ALU.mult, ALU.mult,
                        accum_out=rst[:, h:h + 1]))
                kb.dve.inc(v.tensor_copy(kb.junkv[:, 0:16], gg[:, 0:16]))
                kb.act.wait(v, act0 + t * 3 + 3)
                for h in range(16):
                    ins = v.tensor_scalar(nrm[:, h * 64:(h + 1) * 64],
                                          gg[:, h * 64:(h + 1) * 64],
                                          rst[:, 32 + h:33 + h], None, ALU.mult)
                kb.dve.inc(ins)
                kb.pe.wait(v, pe0 + t * 3 + 2)
                kb.dve.inc(v.tensor_copy(nfm[:, :], pnt[:, :]))
                kb.pe.wait(v, pe0 + t * 3 + 3)
                kb.dve.inc(v.tensor_tensor(
                    kb.x_res[:, t * D:(t + 1) * D],
                    kb.x_res[:, t * D:(t + 1) * D], pyo[:, :], ALU.add))


def build_store(kb, t_y):
    nc = kb.nc
    with nc.Block("store") as blk:
        @blk.sync
        def _(s):
            kb.out.inc(s.dma_start(
                t_y.ap().rearrange("(t p) c -> p t c", p=128),
                kb.x_res.ap().rearrange("p (t c) -> p t c", c=D)), 16)
            kb.out.wait(s)



def kernel(**inputs):
    in_maps = prep_in_maps(inputs)
    nc = build_nc()
    res = run_bass_kernel_spmd(nc, in_maps, core_ids=list(range(8)))
    out = np.zeros((B, S, D), np.float32)
    for cix in range(8):
        seq, j = cix // 4, cix % 4
        y = res.results[cix]["y"]
        out[seq, j * SLAB:(j + 1) * SLAB] = y[:SLAB]
        out[seq, (7 - j) * SLAB:(8 - j) * SLAB] = y[SLAB:]
    return out
